# version 02
# baseline (speedup 1.0000x reference)
import contextlib
import numpy as np

import concourse.bass as bass
import concourse.mybir as mybir
from concourse.bass_utils import run_bass_kernel_spmd

F32 = mybir.dt.float32
BF16 = mybir.dt.bfloat16
AF = mybir.ActivationFunctionType
ALU = mybir.AluOpType
AX = mybir.AxisListType

DILS = [1, 2, 4, 8]
NORM = float(1.0 / np.sqrt(128.0))
NEGINF = -1.0e30
EPS = 1e-5
NB = 2
T = 2048
NT = 4
NCH = 8

ENGS = ("tensor", "vector", "scalar", "gpsimd", "sync")
ENG_SEM = {"tensor": "T", "vector": "V", "scalar": "S", "gpsimd": "G"}


class Sched:

    AUTO = ("vector", "scalar", "gpsimd")

    def __init__(self):
        self.q = {e: [] for e in ENGS}
        self.tot = {}
        self.waited = {e: {} for e in ENGS}

    def E(self, eng, fn, waits=(), inc=None, n=1, dma=False):
        ws = []
        own = ENG_SEM.get(eng)
        auto = (eng in self.AUTO) and not dma
        if auto:
            prev = self.tot.get(own, 0)
            if prev > 0 and self.waited[eng].get(own, 0) < prev:
                self.waited[eng][own] = prev
                ws.append((own, prev))
            if inc is None:
                inc = own
        for (s, v) in waits:
            if v <= 0:
                continue
            if s == own:
                continue
            if self.waited[eng].get(s, 0) >= v:
                continue
            self.waited[eng][s] = v
            ws.append((s, v))
        if inc is not None:
            self.tot[inc] = self.tot.get(inc, 0) + n
        self.q[eng].append((ws, fn, inc, n))
        return (inc, self.tot[inc]) if inc else None

    def run(self, eng, e, sems):
        for (ws, fn, inc, n) in self.q[eng]:
            for (s, v) in ws:
                e.wait_ge(sems[s], v)
            ins = fn(e)
            if inc is not None:
                ins.then_inc(sems[inc], n)


def ev_max(*evs):
    out = {}
    for ev in evs:
        if not ev:
            continue
        if isinstance(ev, list):
            for (s, v) in ev:
                out[s] = max(out.get(s, 0), v)
        else:
            s, v = ev
            out[s] = max(out.get(s, 0), v)
    return list(out.items())


def build_program(upto="full", attn_level=6):
    nc = bass.Bass("TRN2", target_bir_lowering=False)
    S = Sched()

    X = nc.dram_tensor("X", [NB, 8, 128, T], F32, kind="ExternalInput")
    Y = nc.dram_tensor("Y", [NB, 8, 128, T], F32, kind="ExternalOutput")
    WQ = nc.dram_tensor("WQ", [4, 1024, 128], F32, kind="ExternalInput")
    WK = nc.dram_tensor("WK", [4, 1024, 128], F32, kind="ExternalInput")
    WV = nc.dram_tensor("WV", [4, 1024, 128], F32, kind="ExternalInput")
    WO = nc.dram_tensor("WO", [512, 1024], F32, kind="ExternalInput")
    BO = nc.dram_tensor("BO", [128, 8], F32, kind="ExternalInput")
    BF = nc.dram_tensor("BF", [128, 8], F32, kind="ExternalInput")
    G1T = nc.dram_tensor("G1T", [8, 32, 32], F32, kind="ExternalInput")
    G2T = nc.dram_tensor("G2T", [8, 16, 16], F32, kind="ExternalInput")
    SKA = nc.dram_tensor("SKA", [32, 512], F32, kind="ExternalInput")
    SKB = nc.dram_tensor("SKB", [16, 128], F32, kind="ExternalInput")
    MASKS = nc.dram_tensor("MASKS", [4, 128, 128], F32, kind="ExternalInput")
    IDENT = nc.dram_tensor("IDENT", [128, 128], F32, kind="ExternalInput")
    SROW = nc.dram_tensor("SROW", [4096], F32, kind="Internal")
    MIROW = nc.dram_tensor("MIROW", [4096], BF16, kind="Internal")

    with contextlib.ExitStack() as st:
        alloc = st.enter_context
        xs = alloc(nc.sbuf_tensor("xs", [128, 8 * T], F32))
        xb = alloc(nc.sbuf_tensor("xb", [128, 8 * T], BF16))
        murep = alloc(nc.sbuf_tensor("murep", [128, T], BF16))
        invrep = alloc(nc.sbuf_tensor("invrep", [128, T], BF16))
        xbsq = alloc(nc.sbuf_tensor("xbsq", [128, 8 * 512], BF16))
        srowt = alloc(nc.sbuf_tensor("srowt", [33, 1024], F32))
        stats_tc = alloc(nc.sbuf_tensor("stats_tc", [128, 32], F32))
        mu_tc = alloc(nc.sbuf_tensor("mu_tc", [128, 16], F32))
        tmp_tc = alloc(nc.sbuf_tensor("tmp_tc", [128, 16], F32))
        var_tc = alloc(nc.sbuf_tensor("var_tc", [128, 16], F32))
        inv_tc = alloc(nc.sbuf_tensor("inv_tc", [128, 16], F32))
        mi_bf = alloc(nc.sbuf_tensor("mi_bf", [128, 32], BF16))
        qsb = alloc(nc.sbuf_tensor("qsb", [128, T], BF16))
        ksb = alloc(nc.sbuf_tensor("ksb", [128, T], BF16))
        vsb = alloc(nc.sbuf_tensor("vsb", [128, T], BF16))
        vt_sb = alloc(nc.sbuf_tensor("vt_sb", [128, 1024], BF16))
        sm_sb = alloc(nc.sbuf_tensor("sm_sb", [128, 1024], BF16))
        sc_sb = alloc(nc.sbuf_tensor("sc_sb", [128, 2048], F32))
        nmax = alloc(nc.sbuf_tensor("nmax", [128, 16], F32))
        rsum = alloc(nc.sbuf_tensor("rsum", [128, 16], F32))
        rrec = alloc(nc.sbuf_tensor("rrec", [128, 16], F32))
        cat = alloc(nc.sbuf_tensor("cat", [128, 4 * T], BF16))
        wqkv = alloc(nc.sbuf_tensor("wqkv", [128, 12 * 1024], BF16))
        wo_sb = alloc(nc.sbuf_tensor("wo_sb", [128, 4096], BF16))
        wr_sb = alloc(nc.sbuf_tensor("wr_sb", [128, 2048], BF16))
        wi_sb = alloc(nc.sbuf_tensor("wi_sb", [128, 2048], BF16))
        win_sb = alloc(nc.sbuf_tensor("win_sb", [128, 2048], BF16))
        mask_sb = alloc(nc.sbuf_tensor("mask_sb", [128, 512], F32))
        ident_bf = alloc(nc.sbuf_tensor("ident_bf", [128, 128], BF16))
        ones_bf = alloc(nc.sbuf_tensor("ones_bf", [128, 1], BF16))
        g1t_sb = alloc(nc.sbuf_tensor("g1t_sb", [32, 256], F32))
        g2t_sb = alloc(nc.sbuf_tensor("g2t_sb", [16, 128], F32))
        g1rep = alloc(nc.sbuf_tensor("g1rep", [32, 512], BF16))
        g2rep = alloc(nc.sbuf_tensor("g2rep", [16, 512], BF16))
        ska_bf = alloc(nc.sbuf_tensor("ska_bf", [32, 512], BF16))
        skb_bf = alloc(nc.sbuf_tensor("skb_bf", [16, 128], BF16))
        bo_sb = alloc(nc.sbuf_tensor("bo_sb", [128, 8], F32))
        bf_sb = alloc(nc.sbuf_tensor("bf_sb", [128, 8], F32))

        catf = cat.bitcast(F32)
        identf = sc_sb[:, 0:128]
        wacc = [catf[:, 512:1024], catf[:, 1024:1536], catf[:, 3584:4096],
                sc_sb[:, 512:1024]]
        tmp_bf = sm_sb[:, 0:512]
        a_sb = sm_sb[:, 512:1024]
        b_sb = vt_sb[:, 0:512]
        ska_f = catf[0:32, 512:1024]
        skb_f = catf[0:16, 1024:1152]

        def qstage(p):
            return catf[:, 1536 + 1024 * p: 2560 + 1024 * p]

        P = [alloc(nc.psum_tensor(f"P{i}", [128, 512], F32)) for i in range(6)]
        P6 = alloc(nc.psum_tensor("P6", [128, 1024], BF16))
        P7 = alloc(nc.psum_tensor("P7", [128, 512], F32))

        sems = {n: alloc(nc.semaphore(n))
                for n in ("T", "V", "S", "G", "WS", "WQ0", "WQ1", "OU",
                          "XT0", "XT1", "XT2", "XT3", "ST0", "ST1", "ST2", "ST3")}

        res = {}

        def bank_ev(name):
            return res.get(("bank", name))

        def set_bank(name, ev):
            res[("bank", name)] = ev

        xload_ev = {}

        def xload(b, ec, i, waits=()):
            return S.E("sync", lambda e, b=b, ec=ec, i=i: e.dma_start(
                out=xs[:, 2048 * ec + 512 * i: 2048 * ec + 512 * (i + 1)],
                in_=X[b, ec, :, 512 * i: 512 * (i + 1)]),
                waits=waits, inc=f"XT{i}", n=16)

        for ec in range(NCH):
            xload_ev[(0, ec, 0)] = xload(0, ec, 0)
        for ec in range(NCH):
            xload_ev[(0, ec, 0)] = xload_ev[(0, NCH - 1, 0)]
        g1_ld = S.E("sync", lambda e: e.dma_start(
            out=g1t_sb[:, :].rearrange("p (m b) -> p m b", b=32),
            in_=G1T.transpose([1, 0, 2])), inc="WS", n=16)
        g2_ld = S.E("sync", lambda e: e.dma_start(
            out=g2t_sb[:, :].rearrange("p (m ee) -> p m ee", ee=16),
            in_=G2T.transpose([1, 0, 2])), inc="WS", n=16)
        ska_ld = S.E("sync", lambda e: e.dma_start(out=ska_f, in_=SKA[:, :]), inc="WS", n=16)
        skb_ld = S.E("sync", lambda e: e.dma_start(out=skb_f, in_=SKB[:, :]), inc="WS", n=16)
        bo_ld = S.E("sync", lambda e: e.dma_start(out=bo_sb[:, :], in_=BO[:, :]), inc="WS", n=16)
        bf_ld = S.E("sync", lambda e: e.dma_start(out=bf_sb[:, :], in_=BF[:, :]), inc="WS", n=16)
        mask_ld = None
        for h in range(4):
            mask_ld = S.E("sync", lambda e, h=h: e.dma_start(
                out=mask_sb[:, 128 * h: 128 * (h + 1)], in_=MASKS[h, :, :]), inc="WS", n=16)
        id_ld = S.E("sync", lambda e: e.dma_start(out=identf[:, :], in_=IDENT[:, :]), inc="WS", n=16)
        for i in range(1, NT):
            for ec in range(NCH):
                xload_ev[(0, ec, i)] = xload(0, ec, i)
            for ec in range(NCH):
                xload_ev[(0, ec, i)] = xload_ev[(0, NCH - 1, i)]

        qkv_cast_ev = {}
        wo_cast_ev = None
        qstage_free = [None, None]
        par = 0
        specs = []
        for pi, WSRC in enumerate((WQ, WK, WV)):
            for h in range(4):
                specs.append(((pi * 4 + h) * 1024, WSRC, h))
        for idx, (base, WSRC, h) in enumerate(specs):
            p = par % 2
            ld = S.E("sync", lambda e, WSRC=WSRC, h=h, p=p: e.dma_start(
                out=qstage(p).rearrange("p (ec k) -> p ec k", k=128),
                in_=WSRC[h].rearrange("(ec p) k -> p ec k", p=128)),
                waits=ev_max(qstage_free[p]), inc=f"WQ{p}", n=16)
            eng = "vector" if idx % 2 == 0 else "gpsimd"
            cev = S.E(eng, lambda e, base=base, p=p: e.tensor_copy(
                out=wqkv[:, base: base + 1024], in_=qstage(p)),
                waits=ev_max(ld), inc=ENG_SEM[eng])
            qkv_cast_ev[base] = cev
            qstage_free[p] = cev
            par += 1
        for hc in range(4):
            p = par % 2
            ld = S.E("sync", lambda e, hc=hc, p=p: e.dma_start(
                out=qstage(p), in_=WO[128 * hc: 128 * (hc + 1), :]),
                waits=ev_max(qstage_free[p]), inc=f"WQ{p}", n=16)
            wo_cast_ev = S.E("vector", lambda e, hc=hc, p=p: e.tensor_copy(
                out=wo_sb[:, 1024 * hc: 1024 * (hc + 1)], in_=qstage(p)),
                waits=ev_max(ld), inc="V")
            qstage_free[p] = wo_cast_ev
            par += 1

        ones_ev = S.E("gpsimd", lambda e: e.memset(ones_bf[:, :], 1.0), inc="G")
        ska_ev = S.E("vector", lambda e: e.tensor_copy(out=ska_bf[:, :], in_=ska_f),
                     waits=ev_max(ska_ld), inc="V")
        skb_ev = S.E("vector", lambda e: e.tensor_copy(out=skb_bf[:, :], in_=skb_f),
                     waits=ev_max(skb_ld), inc="V")
        id_ev = S.E("gpsimd", lambda e: e.tensor_copy(out=ident_bf[:, :], in_=identf[:, :]),
                    waits=ev_max(id_ld), inc="G")
        bfs_ev = S.E("vector", lambda e: e.tensor_scalar_mul(
            out=bf_sb[:, :], in0=bf_sb[:, :], scalar1=0.5), waits=ev_max(bf_ld), inc="V")

        ffnw_evs = []
        rep_guard = {0: None, 1: None}
        ab_guard = {0: None, 1: None}
        wb = [0]
        for hf in range(2):
            prods = {ic: [] for ic in range(4)}
            for c in range(4):
                m = hf * 4 + c
                g1r = S.E("vector", lambda e, m=m: e.tensor_copy(
                    out=g1rep[:, :].rearrange("p (b ee) -> p b ee", ee=16),
                    in_=g1t_sb[:, 32 * m: 32 * (m + 1)].unsqueeze(2).to_broadcast([32, 32, 16])),
                    waits=ev_max(g1_ld, rep_guard[0]), inc="V")
                g2r = S.E("vector", lambda e, m=m: e.tensor_copy(
                    out=g2rep[:, :].rearrange("p (b ee) -> p b ee", ee=16),
                    in_=g2t_sb[:, 16 * m: 16 * (m + 1)].unsqueeze(1).to_broadcast([16, 32, 16])),
                    waits=ev_max(g2_ld, rep_guard[1]), inc="V")
                pb = 2 + (wb[0] % 2)
                bmm = S.E("tensor", lambda e, pb=pb: e.matmul(
                    P[pb][:, :], skb_bf[:, :], g2rep[:, :], start=True, stop=True),
                    waits=ev_max(skb_ev, g2r, bank_ev(pb)), inc="T")
                bev = S.E("scalar", lambda e, pb=pb: e.copy(out=b_sb, in_=P[pb][:, :]),
                          waits=ev_max(bmm, ab_guard[1]), inc="S")
                set_bank(pb, bev)
                rep_guard[1] = bmm
                amm_last = None
                aev = {}
                for ic in range(4):
                    pa = wb[0] % 2
                    wb[0] += 1
                    amm = S.E("tensor", lambda e, pa=pa, ic=ic: e.matmul(
                        P[pa][:, :], ska_bf[:, 128 * ic: 128 * (ic + 1)], g1rep[:, :],
                        start=True, stop=True),
                        waits=ev_max(ska_ev, g1r, bank_ev(pa)), inc="T")
                    amm_last = amm
                    aev[ic] = S.E("scalar", lambda e, pa=pa: e.copy(out=a_sb, in_=P[pa][:, :]),
                                  waits=ev_max(amm, ab_guard[0]), inc="S")
                    set_bank(pa, aev[ic])
                    if c == 0:
                        pr = S.E("vector", lambda e, ic=ic: e.tensor_mul(
                            out=wacc[ic], in0=a_sb, in1=b_sb),
                            waits=ev_max(aev[ic], bev, res.get(("wacc", ic))), inc="V")
                    else:
                        S.E("vector", lambda e: e.tensor_mul(out=tmp_bf, in0=a_sb, in1=b_sb),
                            waits=ev_max(aev[ic], bev), inc="V")
                        pr = S.E("vector", lambda e, ic=ic: e.tensor_add(
                            out=wacc[ic], in0=wacc[ic], in1=tmp_bf), inc="V")
                    prods[ic].append(pr)
                    ab_guard[0] = pr
                ab_guard[1] = prods[3][-1] if c > 0 else pr
                rep_guard[0] = amm_last
            for ic in range(4):
                dst = wr_sb if hf == 0 else wi_sb
                cev = S.E("vector", lambda e, dst=dst, ic=ic: e.tensor_scalar_mul(
                    out=dst[:, 512 * ic: 512 * (ic + 1)], in0=wacc[ic], scalar1=0.5),
                    waits=ev_max(prods[ic][-1]), inc="V")
                ffnw_evs.append(cev)
                if hf == 1:
                    cev = S.E("vector", lambda e, ic=ic: e.tensor_scalar_mul(
                        out=win_sb[:, 512 * ic: 512 * (ic + 1)], in0=wacc[ic], scalar1=-0.5),
                        inc="V")
                    ffnw_evs.append(cev)
                res[("wacc", ic)] = cev
        wcast_ev = ev_max(*ffnw_evs)
        catfree = ev_max(qstage_free[0], qstage_free[1], wcast_ev)
        ffnw_ev = wcast_ev
        res[("vts", 0)] = wcast_ev
        res[("sm", 0)] = wcast_ev
        res[("sm", 1)] = wcast_ev
        res[("sc", 0)] = id_ev
        res[("sc", 1)] = wcast_ev

        xev = {}

        def ln_phase(b, ln_idx):
            for i in range(NT):
                cast_ev = {}
                sq_ev = {}
                for ec in range(NCH):
                    eng = "vector" if ec % 2 == 0 else "gpsimd"
                    w = ev_max(xload_ev.get((b, ec, i)) if ln_idx == 0 else xev.get(("x", ec, i)),
                               res.get(("xb", ec, i)))
                    cast_ev[ec] = S.E(eng, lambda e, ec=ec, i=i: e.tensor_copy(
                        out=xb[:, 2048 * ec + 512 * i: 2048 * ec + 512 * (i + 1)],
                        in_=xs[:, 2048 * ec + 512 * i: 2048 * ec + 512 * (i + 1)]),
                        waits=w, inc=ENG_SEM[eng])
                    res[("cast", ec, i)] = cast_ev[ec]
                for ec in range(NCH):
                    sq_ev[ec] = S.E("scalar", lambda e, ec=ec, i=i: e.activation(
                        out=xbsq[:, 512 * ec: 512 * (ec + 1)],
                        in_=xb[:, 2048 * ec + 512 * i: 2048 * ec + 512 * (i + 1)],
                        func=AF.Square),
                        waits=ev_max(cast_ev[ec], res.get(("xbsq", ec))), inc="S")
                bnk = 2 + (i % 2)
                mm_ev = None
                for ec in range(NCH):
                    mm_ev = S.E("tensor", lambda e, ec=ec, i=i, bnk=bnk: e.matmul(
                        P[bnk][0:1, :], ones_bf[:, :],
                        xb[:, 2048 * ec + 512 * i: 2048 * ec + 512 * (i + 1)],
                        start=(ec == 0), stop=(ec == 7)),
                        waits=ev_max(cast_ev[ec], ones_ev, bank_ev(bnk)), inc="T")
                for ec in range(NCH):
                    mm_ev = S.E("tensor", lambda e, ec=ec, bnk=bnk: e.matmul(
                        P[bnk][32:33, :], ones_bf[:, :], xbsq[:, 512 * ec: 512 * (ec + 1)],
                        start=(ec == 0), stop=(ec == 7)),
                        waits=ev_max(sq_ev[ec]), inc="T")
                for ec in range(NCH):
                    res[("xbsq", ec)] = mm_ev
                slot = 512 * (i % 2)
                S.E("scalar", lambda e, bnk=bnk, slot=slot: e.copy(
                    out=srowt[0:1, slot: slot + 512], in_=P[bnk][0:1, :]),
                    waits=ev_max(mm_ev, res.get(("srowt", i % 2))), inc="S")
                sevac = S.E("scalar", lambda e, bnk=bnk, slot=slot: e.copy(
                    out=srowt[32:33, slot: slot + 512], in_=P[bnk][32:33, :]), inc="S")
                set_bank(bnk, sevac)
                S.E("gpsimd", lambda e, i=i, slot=slot: e.dma_start(
                    out=SROW[512 * i: 512 * (i + 1)], in_=srowt[0:1, slot: slot + 512]),
                    waits=ev_max(sevac, res.get(("statchain", i))), inc=f"ST{i}", n=16, dma=True)
                fo = S.E("gpsimd", lambda e, i=i, slot=slot: e.dma_start(
                    out=SROW[2048 + 512 * i: 2048 + 512 * (i + 1)],
                    in_=srowt[32:33, slot: slot + 512]), inc=f"ST{i}", n=16, dma=True)
                res[("srowt", i % 2)] = fo
                S.E("gpsimd", lambda e, i=i: e.dma_start(
                    out=stats_tc[:, 4 * i: 4 * (i + 1)],
                    in_=SROW[512 * i: 512 * (i + 1)].rearrange("(p c) -> p c", c=4)),
                    waits=ev_max(fo), inc=f"ST{i}", n=16, dma=True)
                fi = S.E("gpsimd", lambda e, i=i: e.dma_start(
                    out=stats_tc[:, 16 + 4 * i: 16 + 4 * (i + 1)],
                    in_=SROW[2048 + 512 * i: 2048 + 512 * (i + 1)].rearrange("(p c) -> p c", c=4)),
                    inc=f"ST{i}", n=16, dma=True)
                c0, c1 = 4 * i, 4 * (i + 1)
                S.E("vector", lambda e, c0=c0, c1=c1: e.tensor_scalar_mul(
                    out=mu_tc[:, c0:c1], in0=stats_tc[:, c0:c1], scalar1=1.0 / 1024.0),
                    waits=ev_max(fi), inc="V")
                S.E("vector", lambda e, c0=c0, c1=c1: e.tensor_mul(
                    out=tmp_tc[:, c0:c1], in0=mu_tc[:, c0:c1], in1=mu_tc[:, c0:c1]), inc="V")
                S.E("vector", lambda e, c0=c0, c1=c1: e.tensor_scalar_mul(
                    out=var_tc[:, c0:c1], in0=stats_tc[:, 16 + c0: 16 + c1],
                    scalar1=1.0 / 1024.0), inc="V")
                S.E("vector", lambda e, c0=c0, c1=c1: e.tensor_sub(
                    out=var_tc[:, c0:c1], in0=var_tc[:, c0:c1],
                    in1=tmp_tc[:, c0:c1]), inc="V")
                S.E("vector", lambda e, c0=c0, c1=c1: e.tensor_scalar_add(
                    out=var_tc[:, c0:c1], in0=var_tc[:, c0:c1], scalar1=EPS), inc="V")
                vrec = S.E("vector", lambda e, c0=c0, c1=c1: e.reciprocal(
                    out=tmp_tc[:, c0:c1], in_=var_tc[:, c0:c1]), inc="V")
                ssq = S.E("scalar", lambda e, c0=c0, c1=c1: e.activation(
                    out=inv_tc[:, c0:c1], in_=tmp_tc[:, c0:c1], func=AF.Sqrt),
                    waits=ev_max(vrec), inc="S")
                S.E("vector", lambda e, c0=c0, c1=c1: e.tensor_copy(
                    out=mi_bf[:, c0:c1], in_=mu_tc[:, c0:c1]), waits=ev_max(ssq), inc="V")
                mic = S.E("vector", lambda e, c0=c0, c1=c1: e.tensor_copy(
                    out=mi_bf[:, 16 + c0: 16 + c1], in_=inv_tc[:, c0:c1]), inc="V")
                S.E("gpsimd", lambda e, i=i, c0=c0, c1=c1: e.dma_start(
                    out=MIROW[512 * i: 512 * (i + 1)].rearrange("(p c) -> p c", c=4),
                    in_=mi_bf[:, c0:c1]), waits=ev_max(mic), inc=f"ST{i}", n=16, dma=True)
                uo = S.E("gpsimd", lambda e, i=i, c0=c0, c1=c1: e.dma_start(
                    out=MIROW[2048 + 512 * i: 2048 + 512 * (i + 1)].rearrange("(p c) -> p c", c=4),
                    in_=mi_bf[:, 16 + c0: 16 + c1]), inc=f"ST{i}", n=16, dma=True)
                S.E("gpsimd", lambda e, i=i: e.dma_start(
                    out=murep[:, 512 * i: 512 * (i + 1)],
                    in_=MIROW[512 * i: 512 * (i + 1)].unsqueeze(0).to_broadcast([128, 512])),
                    waits=ev_max(uo, res.get(("repl", i))), inc=f"ST{i}", n=16, dma=True)
                rep = S.E("gpsimd", lambda e, i=i: e.dma_start(
                    out=invrep[:, 512 * i: 512 * (i + 1)],
                    in_=MIROW[2048 + 512 * i: 2048 + 512 * (i + 1)].unsqueeze(0).to_broadcast([128, 512])),
                    inc=f"ST{i}", n=16, dma=True)
                res[("statchain", i)] = rep
                sub_ev = {}
                for ec in range(NCH):
                    eng = "vector" if ec % 2 == 1 else "gpsimd"
                    sub_ev[ec] = S.E(eng, lambda e, ec=ec, i=i: e.tensor_sub(
                        out=xb[:, 2048 * ec + 512 * i: 2048 * ec + 512 * (i + 1)],
                        in0=xb[:, 2048 * ec + 512 * i: 2048 * ec + 512 * (i + 1)],
                        in1=murep[:, 512 * i: 512 * (i + 1)]),
                        waits=ev_max(rep, mm_ev, sq_ev[ec]), inc=ENG_SEM[eng])
                mul_evs = []
                for ec in range(NCH):
                    eng = "gpsimd" if ec % 2 == 1 else "vector"
                    mev = S.E(eng, lambda e, ec=ec, i=i: e.tensor_mul(
                        out=xb[:, 2048 * ec + 512 * i: 2048 * ec + 512 * (i + 1)],
                        in0=xb[:, 2048 * ec + 512 * i: 2048 * ec + 512 * (i + 1)],
                        in1=invrep[:, 512 * i: 512 * (i + 1)]),
                        waits=ev_max(rep, sub_ev[ec]), inc=ENG_SEM[eng])
                    res[("xln", ec, i)] = mev
                    mul_evs.append(mev)
                res[("repl", i)] = ev_max(*(list(sub_ev.values()) + mul_evs))

        pc = [0]

        def ffn_phase(b):
            for i in range(NT):
                for oc in range(NCH):
                    bnk = pc[0] % 2
                    pc[0] += 1
                    mm_ev = None
                    for ic in range(NCH):
                        if oc < 4:
                            stat = (wr_sb[:, 512 * ic + 128 * oc: 512 * ic + 128 * (oc + 1)]
                                    if ic < 4 else
                                    win_sb[:, 512 * (ic - 4) + 128 * oc: 512 * (ic - 4) + 128 * (oc + 1)])
                        else:
                            stat = (wi_sb[:, 512 * ic + 128 * (oc - 4): 512 * ic + 128 * (oc - 3)]
                                    if ic < 4 else
                                    wr_sb[:, 512 * (ic - 4) + 128 * (oc - 4): 512 * (ic - 4) + 128 * (oc - 3)])
                        w = ev_max(res.get(("xln", ic, i)), bank_ev(bnk), ffnw_ev)
                        mm_ev = S.E("tensor", lambda e, stat=stat, ic=ic, i=i, bnk=bnk,
                                    first=(ic == 0), last=(ic == 7): e.matmul(
                                        P[bnk][:, :], stat,
                                        xb[:, 2048 * ic + 512 * i: 2048 * ic + 512 * (i + 1)],
                                        start=first, stop=last),
                                    waits=w, inc="T")
                        res[("xb", ic, i)] = mm_ev
                    eev = S.E("vector", lambda e, oc=oc, i=i, bnk=bnk: e.scalar_tensor_tensor(
                        out=xs[:, 2048 * oc + 512 * i: 2048 * oc + 512 * (i + 1)],
                        in0=P[bnk][:, :], scalar=bf_sb[:, oc: oc + 1],
                        in1=xs[:, 2048 * oc + 512 * i: 2048 * oc + 512 * (i + 1)],
                        op0=ALU.add, op1=ALU.add),
                        waits=ev_max(mm_ev, bfs_ev, xev.get(("x", oc, i)),
                                     res.get(("cast", oc, i))), inc="V")
                    set_bank(bnk, eev)
                    xev[("x", oc, i)] = eev

        def attn_phase(b):
            catz_w = ev_max(catfree, res.get("outproj_mm"))
            catz = [S.E("gpsimd", lambda e, h=h: e.memset(
                cat[:, 2048 * h: 2048 * (h + 1)], 0.0), waits=catz_w, inc="G")
                for h in range(4)]
            catz_ev = ev_max(*catz)
            ln_done = [("V", S.tot.get("V", 0)), ("G", S.tot.get("G", 0))]
            cat_evs = []
            last_proj_mm = None
            for h, dil in enumerate(DILS):
                L = 16 // dil
                J = 128 // L
                cols_h = 128 * L
                nsub = max(1, cols_h // 512)
                qkv_done = []
                for pi, dst in enumerate((qsb, ksb, vsb)):
                    base = (pi * 4 + h) * 1024
                    for j in range(nsub):
                        ncols = min(512, cols_h)
                        nl = ncols // 128
                        bnk = pc[0] % 2
                        pc[0] += 1
                        mm_ev = None
                        for ec in range(NCH):
                            xv = xb[:, 2048 * ec: 2048 * (ec + 1)].rearrange(
                                "p (l s) -> p l s", s=128)[:, 4 * j * dil: (4 * j + nl) * dil: dil, :]
                            w = ev_max(qkv_cast_ev[base], bank_ev(bnk), ln_done)
                            mm_ev = S.E("tensor", lambda e, base=base, ec=ec, xv=xv, bnk=bnk,
                                        ncols=ncols, first=(ec == 0), last=(ec == 7): e.matmul(
                                            P[bnk][:, 0:ncols],
                                            wqkv[:, base + 128 * ec: base + 128 * (ec + 1)],
                                            xv, start=first, stop=last),
                                        waits=w, inc="T")
                        last_proj_mm = mm_ev
                        dsv = dst[:, 0:cols_h].rearrange(
                            "p (k jj l) -> p k jj l", k=L, jj=J, l=L
                        ).transpose([0, 3, 1, 2])[:, 4 * j: 4 * j + nl, :, :]
                        psv = P[bnk][:, 0:ncols].rearrange(
                            "p (l k jj) -> p l k jj", k=L, jj=J)
                        eev = S.E("scalar", lambda e, dsv=dsv, psv=psv: e.copy(
                            out=dsv, in_=psv),
                            waits=ev_max(mm_ev, res.get("qkvfree")), inc="S")
                        set_bank(bnk, eev)
                        qkv_done.append(eev)
                qkv_ev = ev_max(*qkv_done)

                ngrp = (L + 3) // 4 if attn_level >= 2 else 0
                for g in range(ngrp):
                    ks = list(range(4 * g, min(4 * g + 4, L)))
                    nk = len(ks)
                    gp = g % 2
                    sc_bnk = 4 + gp
                    slot = 512 * gp
                    kaps = {k: ksb[:, 128 * k: 128 * (k + 1)] for k in ks}
                    qaps = {k: qsb[:, 128 * k: 128 * (k + 1)] for k in ks}
                    vaps = {k: vsb[:, 128 * k: 128 * (k + 1)] for k in ks}
                    smm = None
                    for kk, k in enumerate(ks):
                        smm = S.E("tensor", lambda e, kap=kaps[k], qap=qaps[k], kk=kk,
                                  sc_bnk=sc_bnk: e.matmul(
                                      P[sc_bnk][:, 128 * kk: 128 * (kk + 1)], kap, qap,
                                      start=True, stop=True),
                                  waits=ev_max(qkv_ev, bank_ev(sc_bnk)), inc="T")
                    vtb = P6 if gp == 0 else P[2].bitcast(BF16)
                    vmm = {}
                    if attn_level >= 3:
                        for kk, k in enumerate(ks):
                            vmm[k] = S.E("tensor", lambda e, vap=vaps[k], kk=kk, vtb=vtb: e.transpose(
                                vtb[:, 128 * kk: 128 * (kk + 1)], vap, ident_bf[:, :]),
                                waits=ev_max(id_ev, bank_ev(("vt", gp)), qkv_ev), inc="T")
                    mrep = mask_sb[:, 128 * h: 128 * (h + 1)].unsqueeze(1).to_broadcast([128, nk, 128])
                    scm = S.E("vector", lambda e, slot=slot, nk=nk, sc_bnk=sc_bnk, mrep=mrep:
                              e.scalar_tensor_tensor(
                                  out=sc_sb[:, slot: slot + 128 * nk],
                                  in0=P[sc_bnk][:, 0:128 * nk], scalar=NORM, in1=mrep,
                                  op0=ALU.mult, op1=ALU.add),
                              waits=ev_max(smm, res.get(("sc", gp))), inc="V")
                    set_bank(sc_bnk, scm)
                    rmx = S.E("vector", lambda e, slot=slot, nk=nk, gp=gp: e.reduce_max(
                        out=nmax[:, 4 * gp: 4 * gp + nk],
                        in_=sc_sb[:, slot: slot + 128 * nk].rearrange("p (k f) -> p k f", f=128),
                        axis=AX.X, negate=True),
                        waits=ev_max(res.get(("nmax", gp))), inc="V")
                    exp_evs = []
                    for kk, k in enumerate(ks):
                        exp_evs.append(S.E("scalar", lambda e, slot=slot, kk=kk, gp=gp: e.activation(
                            out=sc_sb[:, 1024 + slot + 128 * kk: 1024 + slot + 128 * (kk + 1)],
                            in_=sc_sb[:, slot + 128 * kk: slot + 128 * (kk + 1)],
                            func=AF.Exp, bias=nmax[:, 4 * gp + kk: 4 * gp + kk + 1], scale=1.0,
                            accum_out=rsum[:, 4 * gp + kk: 4 * gp + kk + 1]),
                            waits=ev_max(rmx, res.get(("exp", gp)), res.get(("rsum", gp))), inc="S"))
                    res[("sc", gp)] = exp_evs[-1]
                    res[("nmax", gp)] = exp_evs[-1]
                    rc = S.E("vector", lambda e, gp=gp, nk=nk: e.reciprocal(
                        out=rrec[:, 4 * gp: 4 * gp + nk], in_=rsum[:, 4 * gp: 4 * gp + nk]),
                        waits=ev_max(*exp_evs, res.get(("rrec", gp))), inc="V")
                    res[("rsum", gp)] = rc
                    sm_evs = {}
                    for kk, k in enumerate(ks):
                        sm_evs[k] = S.E("vector", lambda e, slot=slot, kk=kk, gp=gp:
                                        e.tensor_scalar_mul(
                                            out=sm_sb[:, slot + 128 * kk: slot + 128 * (kk + 1)],
                                            in0=sc_sb[:, 1024 + slot + 128 * kk: 1024 + slot + 128 * (kk + 1)],
                                            scalar1=rrec[:, 4 * gp + kk: 4 * gp + kk + 1]),
                                        waits=ev_max(rc, res.get(("sm", gp))), inc="V")
                    res[("exp", gp)] = ev_max(*sm_evs.values())
                    vte = {}
                    if attn_level >= 3:
                        for kk, k in enumerate(ks):
                            vte[k] = S.E("scalar", lambda e, slot=slot, kk=kk, vtb=vtb: e.copy(
                                out=vt_sb[:, slot + 128 * kk: slot + 128 * (kk + 1)],
                                in_=vtb[:, 128 * kk: 128 * (kk + 1)]),
                                waits=ev_max(vmm[ks[-1]], res.get(("vts", gp))), inc="S")
                        set_bank(("vt", gp), vte[ks[-1]])
                    attb = P7 if gp == 0 else P[3]
                    amm = {}
                    if attn_level < 4:
                        continue
                    for kk, k in enumerate(ks):
                        amm[k] = S.E("tensor", lambda e, slot=slot, kk=kk, attb=attb: e.matmul(
                            attb[:, 128 * kk: 128 * (kk + 1)],
                            vt_sb[:, slot + 128 * kk: slot + 128 * (kk + 1)],
                            sm_sb[:, slot + 128 * kk: slot + 128 * (kk + 1)],
                            start=True, stop=True),
                            waits=ev_max(sm_evs[k], vte[k], bank_ev(("att", gp))), inc="T")
                    res[("vts", gp)] = amm[ks[-1]]
                    res[("sm", gp)] = amm[ks[-1]]
                    res["qkvfree"] = amm[ks[-1]]
                    cev = None
                    for kk, k in enumerate(ks):
                        if attn_level >= 5:
                            catv = cat[:, 2048 * h: 2048 * (h + 1)].rearrange(
                                "p (c r) -> p c r", r=dil)[:, 128 * k: 128 * (k + 1), 0]
                        else:
                            catv = cat[:, 2048 * h + 128 * k: 2048 * h + 128 * (k + 1)]
                        cev = S.E("vector", lambda e, catv=catv, kk=kk, attb=attb: e.tensor_copy(
                            out=catv, in_=attb[:, 128 * kk: 128 * (kk + 1)]),
                            waits=ev_max(amm[ks[-1]], catz_ev), inc="V")
                        cat_evs.append(cev)
                    set_bank(("att", gp), cev)
            for i in range(NT):
                for ec in range(NCH):
                    res[("xb", ec, i)] = last_proj_mm
            cat_done = ev_max(*cat_evs)
            op_mm = None
            for i in range(NT if attn_level >= 6 else 0):
                for oc in range(NCH):
                    bnk = pc[0] % 2
                    pc[0] += 1
                    mm_ev = None
                    for hc in range(4):
                        mm_ev = S.E("tensor", lambda e, hc=hc, oc=oc, i=i, bnk=bnk,
                                    first=(hc == 0), last=(hc == 3): e.matmul(
                                        P[bnk][:, :],
                                        wo_sb[:, 1024 * hc + 128 * oc: 1024 * hc + 128 * (oc + 1)],
                                        cat[:, 2048 * hc + 512 * i: 2048 * hc + 512 * (i + 1)],
                                        start=first, stop=last),
                                    waits=ev_max(cat_done, wo_cast_ev, bank_ev(bnk)), inc="T")
                    op_mm = mm_ev
                    eev = S.E("vector", lambda e, oc=oc, i=i, bnk=bnk: e.scalar_tensor_tensor(
                        out=xs[:, 2048 * oc + 512 * i: 2048 * oc + 512 * (i + 1)],
                        in0=P[bnk][:, :], scalar=bo_sb[:, oc: oc + 1],
                        in1=xs[:, 2048 * oc + 512 * i: 2048 * oc + 512 * (i + 1)],
                        op0=ALU.add, op1=ALU.add),
                        waits=ev_max(mm_ev, bo_ld, xev.get(("x", oc, i)),
                                     res.get(("cast", oc, i))), inc="V")
                    set_bank(bnk, eev)
                    xev[("x", oc, i)] = eev
            if op_mm is not None:
                res["outproj_mm"] = op_mm

        def flush_store():
            allw = [(s, S.tot.get(s, 0)) for s in S.tot]
            for oc in range(NCH):
                S.E("sync", lambda e, oc=oc: e.dma_start(
                    out=Y[0, oc, :, :], in_=xs[:, 2048 * oc: 2048 * (oc + 1)]),
                    waits=allw, inc="OU", n=16)
                S.E("sync", lambda e, oc=oc: e.dma_start(
                    out=Y[1, oc, :, 0:1024], in_=xb.bitcast(F32)[:, 1024 * oc: 1024 * (oc + 1)]),
                    waits=allw, inc="OU", n=16)
            S.E("sync", lambda e: e.dma_start(
                out=Y[1, 0, :, 1024:1056], in_=stats_tc[:, :]), waits=allw, inc="OU", n=16)
            S.E("sync", lambda e: e.dma_start(
                out=Y[1, 0, :, 1056:1072], in_=mu_tc[:, :]), waits=allw, inc="OU", n=16)
            S.E("sync", lambda e: e.dma_start(
                out=Y[1, 0, :, 1072:1088], in_=inv_tc[:, :]), waits=allw, inc="OU", n=16)
            S.E("sync", lambda e: e.dma_start(
                out=Y[1, 0, :, 1088:1104], in_=var_tc[:, :]), waits=allw, inc="OU", n=16)
            S.E("sync", lambda e: e.dma_start(
                out=Y[1, 1, :, 1024:2048], in_=murep.bitcast(F32)[:, :]), waits=allw, inc="OU", n=16)
            S.E("sync", lambda e: e.dma_start(
                out=Y[1, 2, :, 1024:2048], in_=invrep.bitcast(F32)[:, :]), waits=allw, inc="OU", n=16)
            S.E("sync", lambda e: e.dma_start(
                out=Y[1, 3, 0:33, 1024:2048], in_=srowt[:, :]), waits=allw, inc="OU", n=16)
            S.E("sync", lambda e: e.dma_start(
                out=Y[1, 0, :, 1104:1120], in_=tmp_tc[:, :]), waits=allw, inc="OU", n=16)
            S.E("sync", lambda e: e.dma_start(
                out=Y[1, 4, :, 1024:2048], in_=qsb.bitcast(F32)[:, :]), waits=allw, inc="OU", n=16)
            S.E("sync", lambda e: e.dma_start(
                out=Y[1, 5, :, 1024:2048], in_=ksb.bitcast(F32)[:, :]), waits=allw, inc="OU", n=16)
            S.E("sync", lambda e: e.dma_start(
                out=Y[1, 6, :, 1024:2048], in_=vsb.bitcast(F32)[:, :]), waits=allw, inc="OU", n=16)
            S.E("sync", lambda e: e.dma_start(
                out=Y[1, 7, :, 1024:2048], in_=sc_sb[:, 0:1024]), waits=allw, inc="OU", n=16)
            S.E("sync", lambda e: e.dma_start(
                out=Y[1, 1, :, 0:512], in_=sm_sb.bitcast(F32)[:, :]), waits=allw, inc="OU", n=16)
            S.E("sync", lambda e: e.nop(), waits=[("OU", S.tot.get("OU", 0))])

        if upto == "storeb0":
            ln_phase(0, 0)
            ffn_phase(0)
            ln_phase(0, 1)
            attn_phase(0)
            ln_phase(0, 2)
            ffn_phase(0)
            for oc in range(NCH):
                w = ev_max(*[xev[("x", oc, i)] for i in range(NT)])
                S.E("sync", lambda e, oc=oc: e.dma_start(
                    out=Y[0, oc, :, :], in_=xs[:, 2048 * oc: 2048 * (oc + 1)]),
                    waits=w, inc="OU", n=16)
            S.E("sync", lambda e: e.nop(), waits=[("OU", S.tot.get("OU", 0))])
        elif upto.startswith("b1p"):
            nb1 = int(upto[3:])
            ln_phase(0, 0)
            ffn_phase(0)
            ln_phase(0, 1)
            attn_phase(0)
            ln_phase(0, 2)
            ffn_phase(0)
            for oc in range(NCH):
                w = ev_max(*[xev[("x", oc, i)] for i in range(NT)])
                res[("store", oc)] = S.E("sync", lambda e, oc=oc: e.dma_start(
                    out=Y[0, oc, :, :], in_=xs[:, 2048 * oc: 2048 * (oc + 1)]),
                    waits=w, inc="OU", n=16)
            all_st = ev_max(*[res[("store", oc)] for oc in range(NCH)])
            for i in range(NT):
                for ec in range(NCH):
                    xload_ev[(1, ec, i)] = xload(1, ec, i, waits=all_st)
                for ec in range(NCH):
                    xload_ev[(1, ec, i)] = xload_ev[(1, NCH - 1, i)]
            for i in range(NT):
                for ec in range(NCH):
                    xev.pop(("x", ec, i), None)
            ph1 = [lambda: ln_phase(1, 0), lambda: ffn_phase(1), lambda: ln_phase(1, 1),
                   lambda: attn_phase(1), lambda: ln_phase(1, 2), lambda: ffn_phase(1)]
            for fn in ph1[:nb1]:
                fn()
            allw = [(s, S.tot.get(s, 0)) for s in S.tot]
            for oc in range(NCH):
                S.E("sync", lambda e, oc=oc: e.dma_start(
                    out=Y[1, oc, :, :], in_=xs[:, 2048 * oc: 2048 * (oc + 1)]),
                    waits=allw, inc="OU", n=16)
            S.E("sync", lambda e: e.nop(), waits=[("OU", S.tot.get("OU", 0))])
        elif upto != "full":
            seq = {"load": [], "ln1": ["l0"], "ffn1": ["l0", "f"], "ln2": ["l0", "f", "l1"],
                   "attn": ["l0", "f", "l1", "a"], "ln3": ["l0", "f", "l1", "a", "l2"],
                   "ffn2": ["l0", "f", "l1", "a", "l2", "f2"]}[upto]
            for ph in seq:
                if ph == "l0":
                    ln_phase(0, 0)
                elif ph == "f":
                    ffn_phase(0)
                elif ph == "l1":
                    ln_phase(0, 1)
                elif ph == "a":
                    attn_phase(0)
                elif ph == "l2":
                    ln_phase(0, 2)
                elif ph == "f2":
                    ffn_phase(0)
            flush_store()
        else:
          for b in range(NB):
            ln_phase(b, 0)
            ffn_phase(b)
            ln_phase(b, 1)
            attn_phase(b)
            ln_phase(b, 2)
            ffn_phase(b)
            for oc in range(NCH):
                w = ev_max(*[xev[("x", oc, i)] for i in range(NT)])
                res[("store", oc)] = S.E("sync", lambda e, b=b, oc=oc: e.dma_start(
                    out=Y[b, oc, :, :], in_=xs[:, 2048 * oc: 2048 * (oc + 1)]),
                    waits=w, inc="OU", n=16)
            if b + 1 < NB:
                all_st = ev_max(*[res[("store", oc)] for oc in range(NCH)])
                for i in range(NT):
                    for ec in range(NCH):
                        xload_ev[(b + 1, ec, i)] = xload(b + 1, ec, i, waits=all_st)
                    for ec in range(NCH):
                        xload_ev[(b + 1, ec, i)] = xload_ev[(b + 1, NCH - 1, i)]
                for i in range(NT):
                    for ec in range(NCH):
                        xev.pop(("x", ec, i), None)

        S.E("sync", lambda e: e.nop(), waits=[("OU", S.tot.get("OU", 0))])

        with nc.Block() as block:
            @block.tensor
            def _(e):
                S.run("tensor", e, sems)

            @block.vector
            def _(e):
                S.run("vector", e, sems)

            @block.scalar
            def _(e):
                S.run("scalar", e, sems)

            @block.gpsimd
            def _(e):
                S.run("gpsimd", e, sems)

            @block.sync
            def _(e):
                S.run("sync", e, sems)

    return nc


def _host_prep(inputs):
    x = np.asarray(inputs["L0_attn_x"], np.float32)
    Wk = np.asarray(inputs["L0_attn_Wk"], np.float32)
    Wq = np.asarray(inputs["L0_attn_Wq"], np.float32)
    Wv = np.asarray(inputs["L0_attn_Wv"], np.float32)
    W_out = np.asarray(inputs["L0_attn_W_out"], np.float32)
    b_out = np.asarray(inputs["L0_attn_b_out"], np.float32)
    g1r = np.asarray(inputs["L0_ffn_G1_R"], np.float32)
    g2r = np.asarray(inputs["L0_ffn_G2_R"], np.float32)
    g1i = np.asarray(inputs["L0_ffn_G1_I"], np.float32)
    g2i = np.asarray(inputs["L0_ffn_G2_I"], np.float32)
    br = np.asarray(inputs["L0_ffn_bias_real"], np.float32)
    bi = np.asarray(inputs["L0_ffn_bias_imag"], np.float32)

    com = {}
    com["WQ"] = np.ascontiguousarray(Wq.transpose(0, 2, 1))
    com["WK"] = np.ascontiguousarray(Wk.transpose(0, 2, 1))
    com["WV"] = np.ascontiguousarray(Wv.transpose(0, 2, 1))
    com["WO"] = np.ascontiguousarray(W_out.T)
    com["BO"] = np.ascontiguousarray(b_out.reshape(8, 128).T)
    com["BF"] = np.ascontiguousarray(np.concatenate([br, bi]).reshape(8, 128).T)
    com["G1T"] = np.ascontiguousarray(np.concatenate(
        [np.transpose(g1r[0], (1, 2, 0)), np.transpose(g1i[0], (1, 2, 0))], axis=0))
    com["G2T"] = np.ascontiguousarray(np.concatenate(
        [np.transpose(g2r[:, :, 0, :], (0, 2, 1)),
         np.transpose(g2i[:, :, 0, :], (0, 2, 1))], axis=0))
    com["SKA"] = np.kron(np.eye(32, dtype=np.float32), np.ones((1, 16), np.float32))
    com["SKB"] = np.tile(np.eye(16, dtype=np.float32), (1, 8))
    masks = np.empty((4, 128, 128), np.float32)
    for h, dil in enumerate(DILS):
        L = 16 // dil
        p = np.arange(128)
        jp, n = p // L, p % L
        jf, m = p // L, p % L
        allow = (jp[:, None] == jf[None, :]) & (m[None, :] <= n[:, None])
        masks[h] = np.where(allow, np.float32(0.0), np.float32(NEGINF))
    com["MASKS"] = masks
    com["IDENT"] = np.eye(128, dtype=np.float32)

    per_core = []
    for c in range(8):
        m = dict(com)
        m["X"] = np.ascontiguousarray(
            x[2 * c: 2 * c + 2].transpose(0, 2, 1).reshape(NB, 8, 128, T))
        per_core.append(m)
    return per_core


def kernel(**inputs):
    per_core = _host_prep(inputs)
    nc = build_program()
    res = run_bass_kernel_spmd(nc, per_core, core_ids=list(range(8)))
    outs = []
    for c in range(8):
        y = np.asarray(res.results[c]["Y"], np.float32)
        outs.append(y.reshape(NB, 1024, T).transpose(0, 2, 1))
    return np.ascontiguousarray(np.concatenate(outs, axis=0))


# version 03
# speedup vs baseline: 1.1647x; 1.1647x over previous
import contextlib
import numpy as np

import concourse.bass as bass
import concourse.mybir as mybir
from concourse.bass_utils import run_bass_kernel_spmd

F32 = mybir.dt.float32
BF16 = mybir.dt.bfloat16
AF = mybir.ActivationFunctionType
ALU = mybir.AluOpType
AX = mybir.AxisListType

DILS = [1, 2, 4, 8]
NORM = float(1.0 / np.sqrt(128.0))
NEGINF = -1.0e30
EPS = 1e-5
NB = 2
T = 2048
NT = 4
NCH = 8

ENGS = ("tensor", "vector", "scalar", "gpsimd", "sync")
ENG_SEM = {"tensor": "T", "vector": "V", "scalar": "S", "gpsimd": "G"}


class Sched:

    AUTO = ("vector", "scalar", "gpsimd")

    def __init__(self):
        self.q = {e: [] for e in ENGS}
        self.tot = {}
        self.waited = {e: {} for e in ENGS}

    def E(self, eng, fn, waits=(), inc=None, n=1, dma=False):
        ws = []
        own = ENG_SEM.get(eng)
        auto = (eng in self.AUTO) and not dma
        if auto:
            prev = self.tot.get(own, 0)
            if prev > 0 and self.waited[eng].get(own, 0) < prev:
                self.waited[eng][own] = prev
                ws.append((own, prev))
            if inc is None:
                inc = own
        for (s, v) in waits:
            if v <= 0:
                continue
            if s == own:
                continue
            if self.waited[eng].get(s, 0) >= v:
                continue
            self.waited[eng][s] = v
            ws.append((s, v))
        if inc is not None:
            self.tot[inc] = self.tot.get(inc, 0) + n
        self.q[eng].append((ws, fn, inc, n))
        return (inc, self.tot[inc]) if inc else None

    def run(self, eng, e, sems):
        for (ws, fn, inc, n) in self.q[eng]:
            for (s, v) in ws:
                e.wait_ge(sems[s], v)
            ins = fn(e)
            if inc is not None:
                ins.then_inc(sems[inc], n)


def ev_max(*evs):
    out = {}
    for ev in evs:
        if not ev:
            continue
        if isinstance(ev, list):
            for (s, v) in ev:
                out[s] = max(out.get(s, 0), v)
        else:
            s, v = ev
            out[s] = max(out.get(s, 0), v)
    return list(out.items())


def build_program(upto="full", attn_level=6):
    nc = bass.Bass("TRN2", target_bir_lowering=False)
    S = Sched()

    X = nc.dram_tensor("X", [NB, 8, 128, T], F32, kind="ExternalInput")
    Y = nc.dram_tensor("Y", [NB, 8, 128, T], F32, kind="ExternalOutput")
    WQ = nc.dram_tensor("WQ", [4, 1024, 128], F32, kind="ExternalInput")
    WK = nc.dram_tensor("WK", [4, 1024, 128], F32, kind="ExternalInput")
    WV = nc.dram_tensor("WV", [4, 1024, 128], F32, kind="ExternalInput")
    WO = nc.dram_tensor("WO", [512, 1024], F32, kind="ExternalInput")
    BO = nc.dram_tensor("BO", [128, 8], F32, kind="ExternalInput")
    BF = nc.dram_tensor("BF", [128, 8], F32, kind="ExternalInput")
    G1T = nc.dram_tensor("G1T", [8, 32, 32], F32, kind="ExternalInput")
    G2T = nc.dram_tensor("G2T", [8, 16, 16], F32, kind="ExternalInput")
    SKA = nc.dram_tensor("SKA", [32, 512], F32, kind="ExternalInput")
    SKB = nc.dram_tensor("SKB", [16, 128], F32, kind="ExternalInput")
    MASKS = nc.dram_tensor("MASKS", [4, 128, 128], F32, kind="ExternalInput")
    IDENT = nc.dram_tensor("IDENT", [128, 128], F32, kind="ExternalInput")
    SROW = nc.dram_tensor("SROW", [4096], F32, kind="Internal")
    MIROW = nc.dram_tensor("MIROW", [4096], BF16, kind="Internal")

    with contextlib.ExitStack() as st:
        alloc = st.enter_context
        xs = alloc(nc.sbuf_tensor("xs", [128, 8 * T], F32))
        xb = alloc(nc.sbuf_tensor("xb", [128, 8 * T], BF16))
        murep = alloc(nc.sbuf_tensor("murep", [128, T], BF16))
        invrep = alloc(nc.sbuf_tensor("invrep", [128, T], BF16))
        xbsq = alloc(nc.sbuf_tensor("xbsq", [128, 8 * 512], BF16))
        srowt = alloc(nc.sbuf_tensor("srowt", [33, 1024], F32))
        stats_tc = alloc(nc.sbuf_tensor("stats_tc", [128, 32], F32))
        mu_tc = alloc(nc.sbuf_tensor("mu_tc", [128, 16], F32))
        tmp_tc = alloc(nc.sbuf_tensor("tmp_tc", [128, 16], F32))
        var_tc = alloc(nc.sbuf_tensor("var_tc", [128, 16], F32))
        inv_tc = alloc(nc.sbuf_tensor("inv_tc", [128, 16], F32))
        mi_bf = alloc(nc.sbuf_tensor("mi_bf", [128, 32], BF16))
        qsb = alloc(nc.sbuf_tensor("qsb", [128, T], BF16))
        ksb = alloc(nc.sbuf_tensor("ksb", [128, T], BF16))
        vsb = alloc(nc.sbuf_tensor("vsb", [128, T], BF16))
        vt_sb = alloc(nc.sbuf_tensor("vt_sb", [128, 1024], BF16))
        sm_sb = alloc(nc.sbuf_tensor("sm_sb", [128, 1024], BF16))
        sc_sb = alloc(nc.sbuf_tensor("sc_sb", [128, 2048], F32))
        nmax = alloc(nc.sbuf_tensor("nmax", [128, 16], F32))
        rsum = alloc(nc.sbuf_tensor("rsum", [128, 16], F32))
        rrec = alloc(nc.sbuf_tensor("rrec", [128, 16], F32))
        cat = alloc(nc.sbuf_tensor("cat", [128, 4 * T], BF16))
        wqkv = alloc(nc.sbuf_tensor("wqkv", [128, 12 * 1024], BF16))
        wo_sb = alloc(nc.sbuf_tensor("wo_sb", [128, 4096], BF16))
        wr_sb = alloc(nc.sbuf_tensor("wr_sb", [128, 2048], BF16))
        wi_sb = alloc(nc.sbuf_tensor("wi_sb", [128, 2048], BF16))
        win_sb = alloc(nc.sbuf_tensor("win_sb", [128, 2048], BF16))
        mask_sb = alloc(nc.sbuf_tensor("mask_sb", [128, 512], F32))
        ident_bf = alloc(nc.sbuf_tensor("ident_bf", [128, 128], BF16))
        ones_bf = alloc(nc.sbuf_tensor("ones_bf", [128, 1], BF16))
        g1t_sb = alloc(nc.sbuf_tensor("g1t_sb", [32, 256], F32))
        g2t_sb = alloc(nc.sbuf_tensor("g2t_sb", [16, 128], F32))
        g1rep = alloc(nc.sbuf_tensor("g1rep", [32, 512], BF16))
        g2rep = alloc(nc.sbuf_tensor("g2rep", [16, 512], BF16))
        ska_bf = alloc(nc.sbuf_tensor("ska_bf", [32, 512], BF16))
        skb_bf = alloc(nc.sbuf_tensor("skb_bf", [16, 128], BF16))
        bo_sb = alloc(nc.sbuf_tensor("bo_sb", [128, 8], F32))
        bf_sb = alloc(nc.sbuf_tensor("bf_sb", [128, 8], F32))

        catf = cat.bitcast(F32)
        identf = sc_sb[:, 0:128]
        wacc = [catf[:, 512:1024], catf[:, 1024:1536], catf[:, 3584:4096],
                sc_sb[:, 512:1024]]
        tmp_bf = sm_sb[:, 0:512]
        a_sb = sm_sb[:, 512:1024]
        b_sb = vt_sb[:, 0:512]
        ska_f = catf[0:32, 512:1024]
        skb_f = catf[0:16, 1024:1152]

        def qstage(p):
            return catf[:, 1536 + 1024 * p: 2560 + 1024 * p]

        P = [alloc(nc.psum_tensor(f"P{i}", [128, 512], F32)) for i in range(6)]
        P6 = alloc(nc.psum_tensor("P6", [128, 1024], BF16))
        P7 = alloc(nc.psum_tensor("P7", [128, 512], F32))

        sems = {n: alloc(nc.semaphore(n))
                for n in ("T", "V", "S", "G", "WS", "WQ0", "WQ1", "OU",
                          "XT0", "XT1", "XT2", "XT3", "ST0", "ST1", "ST2", "ST3")}

        res = {}

        def bank_ev(name):
            return res.get(("bank", name))

        def set_bank(name, ev):
            res[("bank", name)] = ev

        xload_ev = {}

        def xload(b, ec, i, waits=(), q="sync"):
            return S.E(q, lambda e, b=b, ec=ec, i=i: e.dma_start(
                out=xs[:, 2048 * ec + 512 * i: 2048 * ec + 512 * (i + 1)],
                in_=X[b, ec, :, 512 * i: 512 * (i + 1)]),
                waits=waits, inc=f"XT{i}", n=16, dma=True)

        for ec in range(NCH):
            xload_ev[(0, ec, 0)] = xload(0, ec, 0)
        for ec in range(NCH):
            xload_ev[(0, ec, 0)] = xload_ev[(0, NCH - 1, 0)]
        g1_ld = S.E("sync", lambda e: e.dma_start(
            out=g1t_sb[:, :].rearrange("p (m b) -> p m b", b=32),
            in_=G1T.transpose([1, 0, 2])), inc="WS", n=16)
        g2_ld = S.E("sync", lambda e: e.dma_start(
            out=g2t_sb[:, :].rearrange("p (m ee) -> p m ee", ee=16),
            in_=G2T.transpose([1, 0, 2])), inc="WS", n=16)
        ska_ld = S.E("sync", lambda e: e.dma_start(out=ska_f, in_=SKA[:, :]), inc="WS", n=16)
        skb_ld = S.E("sync", lambda e: e.dma_start(out=skb_f, in_=SKB[:, :]), inc="WS", n=16)
        bo_ld = S.E("sync", lambda e: e.dma_start(out=bo_sb[:, :], in_=BO[:, :]), inc="WS", n=16)
        bf_ld = S.E("sync", lambda e: e.dma_start(out=bf_sb[:, :], in_=BF[:, :]), inc="WS", n=16)
        mask_ld = None
        for h in range(4):
            mask_ld = S.E("sync", lambda e, h=h: e.dma_start(
                out=mask_sb[:, 128 * h: 128 * (h + 1)], in_=MASKS[h, :, :]), inc="WS", n=16)
        id_ld = S.E("sync", lambda e: e.dma_start(out=identf[:, :], in_=IDENT[:, :]), inc="WS", n=16)
        for i in range(1, NT):
            for ec in range(NCH):
                xload_ev[(0, ec, i)] = xload(0, ec, i, q="scalar")
            for ec in range(NCH):
                xload_ev[(0, ec, i)] = xload_ev[(0, NCH - 1, i)]

        qkv_cast_ev = {}
        wo_cast_ev = None
        qstage_free = [None, None]
        par = 0
        specs = []
        for pi, WSRC in enumerate((WQ, WK, WV)):
            for h in range(4):
                specs.append(((pi * 4 + h) * 1024, WSRC, h))
        for idx, (base, WSRC, h) in enumerate(specs):
            p = par % 2
            ld = S.E("sync", lambda e, WSRC=WSRC, h=h, p=p: e.dma_start(
                out=qstage(p).rearrange("p (ec k) -> p ec k", k=128),
                in_=WSRC[h].rearrange("(ec p) k -> p ec k", p=128)),
                waits=ev_max(qstage_free[p]), inc=f"WQ{p}", n=16)
            eng = "vector" if idx % 2 == 0 else "gpsimd"
            cev = S.E(eng, lambda e, base=base, p=p: e.tensor_copy(
                out=wqkv[:, base: base + 1024], in_=qstage(p)),
                waits=ev_max(ld), inc=ENG_SEM[eng])
            qkv_cast_ev[base] = cev
            qstage_free[p] = cev
            par += 1
        for hc in range(4):
            p = par % 2
            ld = S.E("sync", lambda e, hc=hc, p=p: e.dma_start(
                out=qstage(p), in_=WO[128 * hc: 128 * (hc + 1), :]),
                waits=ev_max(qstage_free[p]), inc=f"WQ{p}", n=16)
            wo_cast_ev = S.E("vector", lambda e, hc=hc, p=p: e.tensor_copy(
                out=wo_sb[:, 1024 * hc: 1024 * (hc + 1)], in_=qstage(p)),
                waits=ev_max(ld), inc="V")
            qstage_free[p] = wo_cast_ev
            par += 1

        ones_ev = S.E("gpsimd", lambda e: e.memset(ones_bf[:, :], 1.0), inc="G")
        ska_ev = S.E("vector", lambda e: e.tensor_copy(out=ska_bf[:, :], in_=ska_f),
                     waits=ev_max(ska_ld), inc="V")
        skb_ev = S.E("vector", lambda e: e.tensor_copy(out=skb_bf[:, :], in_=skb_f),
                     waits=ev_max(skb_ld), inc="V")
        id_ev = S.E("gpsimd", lambda e: e.tensor_copy(out=ident_bf[:, :], in_=identf[:, :]),
                    waits=ev_max(id_ld), inc="G")
        bfs_ev = S.E("vector", lambda e: e.tensor_scalar_mul(
            out=bf_sb[:, :], in0=bf_sb[:, :], scalar1=0.5), waits=ev_max(bf_ld), inc="V")

        ffnw_evs = []
        rep_guard = {0: None, 1: None}
        ab_guard = {0: None, 1: None}
        wb = [0]
        for hf in range(2):
            prods = {ic: [] for ic in range(4)}
            for c in range(4):
                m = hf * 4 + c
                g1r = S.E("vector", lambda e, m=m: e.tensor_copy(
                    out=g1rep[:, :].rearrange("p (b ee) -> p b ee", ee=16),
                    in_=g1t_sb[:, 32 * m: 32 * (m + 1)].unsqueeze(2).to_broadcast([32, 32, 16])),
                    waits=ev_max(g1_ld, rep_guard[0]), inc="V")
                g2r = S.E("vector", lambda e, m=m: e.tensor_copy(
                    out=g2rep[:, :].rearrange("p (b ee) -> p b ee", ee=16),
                    in_=g2t_sb[:, 16 * m: 16 * (m + 1)].unsqueeze(1).to_broadcast([16, 32, 16])),
                    waits=ev_max(g2_ld, rep_guard[1]), inc="V")
                pb = 2 + (wb[0] % 2)
                bmm = S.E("tensor", lambda e, pb=pb: e.matmul(
                    P[pb][:, :], skb_bf[:, :], g2rep[:, :], start=True, stop=True),
                    waits=ev_max(skb_ev, g2r, bank_ev(pb)), inc="T")
                bev = S.E("scalar", lambda e, pb=pb: e.copy(out=b_sb, in_=P[pb][:, :]),
                          waits=ev_max(bmm, ab_guard[1]), inc="S")
                set_bank(pb, bev)
                rep_guard[1] = bmm
                amm_last = None
                aev = {}
                for ic in range(4):
                    pa = wb[0] % 2
                    wb[0] += 1
                    amm = S.E("tensor", lambda e, pa=pa, ic=ic: e.matmul(
                        P[pa][:, :], ska_bf[:, 128 * ic: 128 * (ic + 1)], g1rep[:, :],
                        start=True, stop=True),
                        waits=ev_max(ska_ev, g1r, bank_ev(pa)), inc="T")
                    amm_last = amm
                    aev[ic] = S.E("scalar", lambda e, pa=pa: e.copy(out=a_sb, in_=P[pa][:, :]),
                                  waits=ev_max(amm, ab_guard[0]), inc="S")
                    set_bank(pa, aev[ic])
                    if c == 0:
                        pr = S.E("vector", lambda e, ic=ic: e.tensor_mul(
                            out=wacc[ic], in0=a_sb, in1=b_sb),
                            waits=ev_max(aev[ic], bev, res.get(("wacc", ic))), inc="V")
                    else:
                        S.E("vector", lambda e: e.tensor_mul(out=tmp_bf, in0=a_sb, in1=b_sb),
                            waits=ev_max(aev[ic], bev), inc="V")
                        pr = S.E("vector", lambda e, ic=ic: e.tensor_add(
                            out=wacc[ic], in0=wacc[ic], in1=tmp_bf), inc="V")
                    prods[ic].append(pr)
                    ab_guard[0] = pr
                ab_guard[1] = prods[3][-1] if c > 0 else pr
                rep_guard[0] = amm_last
            for ic in range(4):
                dst = wr_sb if hf == 0 else wi_sb
                cev = S.E("vector", lambda e, dst=dst, ic=ic: e.tensor_scalar_mul(
                    out=dst[:, 512 * ic: 512 * (ic + 1)], in0=wacc[ic], scalar1=0.5),
                    waits=ev_max(prods[ic][-1]), inc="V")
                ffnw_evs.append(cev)
                if hf == 1:
                    cev = S.E("vector", lambda e, ic=ic: e.tensor_scalar_mul(
                        out=win_sb[:, 512 * ic: 512 * (ic + 1)], in0=wacc[ic], scalar1=-0.5),
                        inc="V")
                    ffnw_evs.append(cev)
                res[("wacc", ic)] = cev
        wcast_ev = ev_max(*ffnw_evs)
        catfree = ev_max(qstage_free[0], qstage_free[1], wcast_ev)
        ffnw_ev = wcast_ev
        res[("vts", 0)] = wcast_ev
        res[("sm", 0)] = wcast_ev
        res[("sm", 1)] = wcast_ev
        res[("sc", 0)] = id_ev
        res[("sc", 1)] = wcast_ev

        xev = {}

        def ln_phase(b, ln_idx):
            cast_ev = {}
            for ec in range(NCH):
                eng = "vector" if ec % 2 == 0 else "gpsimd"
                w = []
                for i in range(NT):
                    if ln_idx == 0:
                        w += ev_max(xload_ev.get((b, ec, i)))
                    else:
                        w += ev_max(xev.get(("x", ec, i)))
                    w += ev_max(res.get(("xb", ec, i)))
                w = ev_max(w)
                cast_ev[ec] = S.E(eng, lambda e, ec=ec: e.tensor_copy(
                    out=xb[:, 2048 * ec: 2048 * (ec + 1)],
                    in_=xs[:, 2048 * ec: 2048 * (ec + 1)]),
                    waits=w, inc=ENG_SEM[eng])
                for i in range(NT):
                    res[("cast", ec, i)] = cast_ev[ec]
            allcast = ev_max(*cast_ev.values())
            fi_evs = []
            mm_last = None
            for i in range(NT):
                sq = S.E("scalar", lambda e, i=i: e.activation(
                    out=xbsq[:, :],
                    in_=bass.AP(xb, 512 * i, [[16384, 128], [2048, 8], [1, 512]]),
                    func=AF.Square),
                    waits=ev_max(allcast, res.get("xbsq")), inc="S")
                bnk = 2 + (i % 2)
                mm_ev = None
                for ec in range(NCH):
                    mm_ev = S.E("tensor", lambda e, ec=ec, i=i, bnk=bnk: e.matmul(
                        P[bnk][0:1, :], ones_bf[:, :],
                        xb[:, 2048 * ec + 512 * i: 2048 * ec + 512 * (i + 1)],
                        start=(ec == 0), stop=(ec == 7)),
                        waits=ev_max(cast_ev[ec], ones_ev, bank_ev(bnk)), inc="T")
                for ec in range(NCH):
                    mm_ev = S.E("tensor", lambda e, ec=ec, bnk=bnk: e.matmul(
                        P[bnk][32:33, :], ones_bf[:, :], xbsq[:, 512 * ec: 512 * (ec + 1)],
                        start=(ec == 0), stop=(ec == 7)),
                        waits=ev_max(sq), inc="T")
                res["xbsq"] = mm_ev
                mm_last = mm_ev
                slot = 512 * (i % 2)
                S.E("scalar", lambda e, bnk=bnk, slot=slot: e.copy(
                    out=srowt[0:1, slot: slot + 512], in_=P[bnk][0:1, :]),
                    waits=ev_max(mm_ev, res.get(("srowt", i % 2))), inc="S")
                sevac = S.E("scalar", lambda e, bnk=bnk, slot=slot: e.copy(
                    out=srowt[32:33, slot: slot + 512], in_=P[bnk][32:33, :]), inc="S")
                set_bank(bnk, sevac)
                S.E("gpsimd", lambda e, i=i, slot=slot: e.dma_start(
                    out=SROW[512 * i: 512 * (i + 1)], in_=srowt[0:1, slot: slot + 512]),
                    waits=ev_max(sevac, res.get(("statchain", i))), inc=f"ST{i}", n=16, dma=True)
                fo = S.E("gpsimd", lambda e, i=i, slot=slot: e.dma_start(
                    out=SROW[2048 + 512 * i: 2048 + 512 * (i + 1)],
                    in_=srowt[32:33, slot: slot + 512]), inc=f"ST{i}", n=16, dma=True)
                res[("srowt", i % 2)] = fo
                S.E("gpsimd", lambda e, i=i: e.dma_start(
                    out=stats_tc[:, 4 * i: 4 * (i + 1)],
                    in_=SROW[512 * i: 512 * (i + 1)].rearrange("(p c) -> p c", c=4)),
                    waits=ev_max(fo), inc=f"ST{i}", n=16, dma=True)
                fi = S.E("gpsimd", lambda e, i=i: e.dma_start(
                    out=stats_tc[:, 16 + 4 * i: 16 + 4 * (i + 1)],
                    in_=SROW[2048 + 512 * i: 2048 + 512 * (i + 1)].rearrange("(p c) -> p c", c=4)),
                    inc=f"ST{i}", n=16, dma=True)
                fi_evs.append(fi)
            S.E("vector", lambda e: e.tensor_scalar_mul(
                out=mu_tc[:, :], in0=stats_tc[:, 0:16], scalar1=1.0 / 1024.0),
                waits=ev_max(*fi_evs), inc="V")
            S.E("vector", lambda e: e.tensor_mul(
                out=tmp_tc[:, :], in0=mu_tc[:, :], in1=mu_tc[:, :]), inc="V")
            S.E("vector", lambda e: e.tensor_scalar_mul(
                out=var_tc[:, :], in0=stats_tc[:, 16:32], scalar1=1.0 / 1024.0), inc="V")
            S.E("vector", lambda e: e.tensor_sub(
                out=var_tc[:, :], in0=var_tc[:, :], in1=tmp_tc[:, :]), inc="V")
            S.E("vector", lambda e: e.tensor_scalar_add(
                out=var_tc[:, :], in0=var_tc[:, :], scalar1=EPS), inc="V")
            vrec = S.E("vector", lambda e: e.reciprocal(
                out=tmp_tc[:, :], in_=var_tc[:, :]), inc="V")
            ssq = S.E("scalar", lambda e: e.activation(
                out=inv_tc[:, :], in_=tmp_tc[:, :], func=AF.Sqrt),
                waits=ev_max(vrec), inc="S")
            S.E("vector", lambda e: e.tensor_copy(
                out=mi_bf[:, 0:16], in_=mu_tc[:, :]), waits=ev_max(ssq), inc="V")
            mic = S.E("vector", lambda e: e.tensor_copy(
                out=mi_bf[:, 16:32], in_=inv_tc[:, :]), inc="V")
            S.E("gpsimd", lambda e: e.dma_start(
                out=bass.AP(MIROW, 0, [[4, 128], [512, 4], [1, 4]]),
                in_=mi_bf[:, 0:16].rearrange("p (i c) -> p i c", c=4)),
                waits=ev_max(mic), inc="ST0", n=16, dma=True)
            uo = S.E("gpsimd", lambda e: e.dma_start(
                out=bass.AP(MIROW, 2048, [[4, 128], [512, 4], [1, 4]]),
                in_=mi_bf[:, 16:32].rearrange("p (i c) -> p i c", c=4)),
                inc="ST0", n=16, dma=True)
            S.E("gpsimd", lambda e: e.dma_start(
                out=murep[:, :],
                in_=MIROW[0:2048].unsqueeze(0).to_broadcast([128, 2048])),
                waits=ev_max(uo, res.get("repl")), inc="ST0", n=16, dma=True)
            rep = S.E("gpsimd", lambda e: e.dma_start(
                out=invrep[:, :],
                in_=MIROW[2048:4096].unsqueeze(0).to_broadcast([128, 2048])),
                inc="ST0", n=16, dma=True)
            for i in range(NT):
                res[("statchain", i)] = rep
            sub_ev = {}
            for ec in range(NCH):
                eng = "vector" if ec % 2 == 1 else "gpsimd"
                sub_ev[ec] = S.E(eng, lambda e, ec=ec: e.tensor_sub(
                    out=xb[:, 2048 * ec: 2048 * (ec + 1)],
                    in0=xb[:, 2048 * ec: 2048 * (ec + 1)], in1=murep[:, :]),
                    waits=ev_max(rep, mm_last), inc=ENG_SEM[eng])
            mul_evs = []
            for ec in range(NCH):
                eng = "gpsimd" if ec % 2 == 1 else "vector"
                mev = S.E(eng, lambda e, ec=ec: e.tensor_mul(
                    out=xb[:, 2048 * ec: 2048 * (ec + 1)],
                    in0=xb[:, 2048 * ec: 2048 * (ec + 1)], in1=invrep[:, :]),
                    waits=ev_max(rep, sub_ev[ec]), inc=ENG_SEM[eng])
                for i in range(NT):
                    res[("xln", ec, i)] = mev
                mul_evs.append(mev)
            res["repl"] = ev_max(*(list(sub_ev.values()) + mul_evs))

        pc = [0]

        def ffn_phase(b):
            for i in range(NT):
                for oc in range(NCH):
                    bnk = pc[0] % 2
                    pc[0] += 1
                    mm_ev = None
                    for ic in range(NCH):
                        if oc < 4:
                            stat = (wr_sb[:, 512 * ic + 128 * oc: 512 * ic + 128 * (oc + 1)]
                                    if ic < 4 else
                                    win_sb[:, 512 * (ic - 4) + 128 * oc: 512 * (ic - 4) + 128 * (oc + 1)])
                        else:
                            stat = (wi_sb[:, 512 * ic + 128 * (oc - 4): 512 * ic + 128 * (oc - 3)]
                                    if ic < 4 else
                                    wr_sb[:, 512 * (ic - 4) + 128 * (oc - 4): 512 * (ic - 4) + 128 * (oc - 3)])
                        w = ev_max(res.get(("xln", ic, i)), bank_ev(bnk), ffnw_ev)
                        mm_ev = S.E("tensor", lambda e, stat=stat, ic=ic, i=i, bnk=bnk,
                                    first=(ic == 0), last=(ic == 7): e.matmul(
                                        P[bnk][:, :], stat,
                                        xb[:, 2048 * ic + 512 * i: 2048 * ic + 512 * (i + 1)],
                                        start=first, stop=last),
                                    waits=w, inc="T")
                        res[("xb", ic, i)] = mm_ev
                    eev = S.E("vector", lambda e, oc=oc, i=i, bnk=bnk: e.scalar_tensor_tensor(
                        out=xs[:, 2048 * oc + 512 * i: 2048 * oc + 512 * (i + 1)],
                        in0=P[bnk][:, :], scalar=bf_sb[:, oc: oc + 1],
                        in1=xs[:, 2048 * oc + 512 * i: 2048 * oc + 512 * (i + 1)],
                        op0=ALU.add, op1=ALU.add),
                        waits=ev_max(mm_ev, bfs_ev, xev.get(("x", oc, i)),
                                     res.get(("cast", oc, i))), inc="V")
                    set_bank(bnk, eev)
                    xev[("x", oc, i)] = eev

        def attn_phase(b):
            catz_w = ev_max(catfree, res.get("outproj_mm"))
            catz = [S.E("gpsimd", lambda e, h=h: e.memset(
                cat[:, 2048 * h: 2048 * (h + 1)], 0.0), waits=catz_w, inc="G")
                for h in range(4)]
            catz_ev = ev_max(*catz)
            ln_done = [("V", S.tot.get("V", 0)), ("G", S.tot.get("G", 0))]
            cat_evs = []
            last_proj_mm = None
            for h, dil in enumerate(DILS):
                L = 16 // dil
                J = 128 // L
                cols_h = 128 * L
                nsub = max(1, cols_h // 512)
                qkv_done = []
                for pi, dst in enumerate((qsb, ksb, vsb)):
                    base = (pi * 4 + h) * 1024
                    for j in range(nsub):
                        ncols = min(512, cols_h)
                        nl = ncols // 128
                        bnk = pc[0] % 2
                        pc[0] += 1
                        mm_ev = None
                        for ec in range(NCH):
                            xv = xb[:, 2048 * ec: 2048 * (ec + 1)].rearrange(
                                "p (l s) -> p l s", s=128)[:, 4 * j * dil: (4 * j + nl) * dil: dil, :]
                            w = ev_max(qkv_cast_ev[base], bank_ev(bnk), ln_done)
                            mm_ev = S.E("tensor", lambda e, base=base, ec=ec, xv=xv, bnk=bnk,
                                        ncols=ncols, first=(ec == 0), last=(ec == 7): e.matmul(
                                            P[bnk][:, 0:ncols],
                                            wqkv[:, base + 128 * ec: base + 128 * (ec + 1)],
                                            xv, start=first, stop=last),
                                        waits=w, inc="T")
                        last_proj_mm = mm_ev
                        dsv = dst[:, 0:cols_h].rearrange(
                            "p (k jj l) -> p k jj l", k=L, jj=J, l=L
                        ).transpose([0, 3, 1, 2])[:, 4 * j: 4 * j + nl, :, :]
                        psv = P[bnk][:, 0:ncols].rearrange(
                            "p (l k jj) -> p l k jj", k=L, jj=J)
                        eev = S.E("scalar", lambda e, dsv=dsv, psv=psv: e.copy(
                            out=dsv, in_=psv),
                            waits=ev_max(mm_ev, res.get("qkvfree")), inc="S")
                        set_bank(bnk, eev)
                        qkv_done.append(eev)
                qkv_ev = ev_max(*qkv_done)

                ngrp = (L + 3) // 4 if attn_level >= 2 else 0
                for g in range(ngrp):
                    ks = list(range(4 * g, min(4 * g + 4, L)))
                    nk = len(ks)
                    gp = g % 2
                    sc_bnk = 4 + gp
                    slot = 512 * gp
                    kaps = {k: ksb[:, 128 * k: 128 * (k + 1)] for k in ks}
                    qaps = {k: qsb[:, 128 * k: 128 * (k + 1)] for k in ks}
                    vaps = {k: vsb[:, 128 * k: 128 * (k + 1)] for k in ks}
                    smm = None
                    for kk, k in enumerate(ks):
                        smm = S.E("tensor", lambda e, kap=kaps[k], qap=qaps[k], kk=kk,
                                  sc_bnk=sc_bnk: e.matmul(
                                      P[sc_bnk][:, 128 * kk: 128 * (kk + 1)], kap, qap,
                                      start=True, stop=True),
                                  waits=ev_max(qkv_ev, bank_ev(sc_bnk)), inc="T")
                    vtb = P6 if gp == 0 else P[2].bitcast(BF16)
                    vmm = {}
                    if attn_level >= 3:
                        for kk, k in enumerate(ks):
                            vmm[k] = S.E("tensor", lambda e, vap=vaps[k], kk=kk, vtb=vtb: e.transpose(
                                vtb[:, 128 * kk: 128 * (kk + 1)], vap, ident_bf[:, :]),
                                waits=ev_max(id_ev, bank_ev(("vt", gp)), qkv_ev), inc="T")
                    mrep = mask_sb[:, 128 * h: 128 * (h + 1)].unsqueeze(1).to_broadcast([128, nk, 128])
                    scm = S.E("vector", lambda e, slot=slot, nk=nk, sc_bnk=sc_bnk, mrep=mrep:
                              e.scalar_tensor_tensor(
                                  out=sc_sb[:, slot: slot + 128 * nk],
                                  in0=P[sc_bnk][:, 0:128 * nk], scalar=NORM, in1=mrep,
                                  op0=ALU.mult, op1=ALU.add),
                              waits=ev_max(smm, res.get(("sc", gp))), inc="V")
                    set_bank(sc_bnk, scm)
                    rmx = S.E("vector", lambda e, slot=slot, nk=nk, gp=gp: e.reduce_max(
                        out=nmax[:, 4 * gp: 4 * gp + nk],
                        in_=sc_sb[:, slot: slot + 128 * nk].rearrange("p (k f) -> p k f", f=128),
                        axis=AX.X, negate=True),
                        waits=ev_max(res.get(("nmax", gp))), inc="V")
                    exp_evs = []
                    for kk, k in enumerate(ks):
                        exp_evs.append(S.E("scalar", lambda e, slot=slot, kk=kk, gp=gp: e.activation(
                            out=sc_sb[:, 1024 + slot + 128 * kk: 1024 + slot + 128 * (kk + 1)],
                            in_=sc_sb[:, slot + 128 * kk: slot + 128 * (kk + 1)],
                            func=AF.Exp, bias=nmax[:, 4 * gp + kk: 4 * gp + kk + 1], scale=1.0,
                            accum_out=rsum[:, 4 * gp + kk: 4 * gp + kk + 1]),
                            waits=ev_max(rmx, res.get(("exp", gp)), res.get(("rsum", gp))), inc="S"))
                    res[("sc", gp)] = exp_evs[-1]
                    res[("nmax", gp)] = exp_evs[-1]
                    rc = S.E("vector", lambda e, gp=gp, nk=nk: e.reciprocal(
                        out=rrec[:, 4 * gp: 4 * gp + nk], in_=rsum[:, 4 * gp: 4 * gp + nk]),
                        waits=ev_max(*exp_evs, res.get(("rrec", gp))), inc="V")
                    res[("rsum", gp)] = rc
                    sm_evs = {}
                    for kk, k in enumerate(ks):
                        sm_evs[k] = S.E("vector", lambda e, slot=slot, kk=kk, gp=gp:
                                        e.tensor_scalar_mul(
                                            out=sm_sb[:, slot + 128 * kk: slot + 128 * (kk + 1)],
                                            in0=sc_sb[:, 1024 + slot + 128 * kk: 1024 + slot + 128 * (kk + 1)],
                                            scalar1=rrec[:, 4 * gp + kk: 4 * gp + kk + 1]),
                                        waits=ev_max(rc, res.get(("sm", gp))), inc="V")
                    res[("exp", gp)] = ev_max(*sm_evs.values())
                    vte = {}
                    if attn_level >= 3:
                        for kk, k in enumerate(ks):
                            vte[k] = S.E("scalar", lambda e, slot=slot, kk=kk, vtb=vtb: e.copy(
                                out=vt_sb[:, slot + 128 * kk: slot + 128 * (kk + 1)],
                                in_=vtb[:, 128 * kk: 128 * (kk + 1)]),
                                waits=ev_max(vmm[ks[-1]], res.get(("vts", gp))), inc="S")
                        set_bank(("vt", gp), vte[ks[-1]])
                    attb = P7 if gp == 0 else P[3]
                    amm = {}
                    if attn_level < 4:
                        continue
                    for kk, k in enumerate(ks):
                        amm[k] = S.E("tensor", lambda e, slot=slot, kk=kk, attb=attb: e.matmul(
                            attb[:, 128 * kk: 128 * (kk + 1)],
                            vt_sb[:, slot + 128 * kk: slot + 128 * (kk + 1)],
                            sm_sb[:, slot + 128 * kk: slot + 128 * (kk + 1)],
                            start=True, stop=True),
                            waits=ev_max(sm_evs[k], vte[k], bank_ev(("att", gp))), inc="T")
                    res[("vts", gp)] = amm[ks[-1]]
                    res[("sm", gp)] = amm[ks[-1]]
                    res["qkvfree"] = amm[ks[-1]]
                    cev = None
                    for kk, k in enumerate(ks):
                        if attn_level >= 5:
                            catv = cat[:, 2048 * h: 2048 * (h + 1)].rearrange(
                                "p (c r) -> p c r", r=dil)[:, 128 * k: 128 * (k + 1), 0]
                        else:
                            catv = cat[:, 2048 * h + 128 * k: 2048 * h + 128 * (k + 1)]
                        cev = S.E("vector", lambda e, catv=catv, kk=kk, attb=attb: e.tensor_copy(
                            out=catv, in_=attb[:, 128 * kk: 128 * (kk + 1)]),
                            waits=ev_max(amm[ks[-1]], catz_ev), inc="V")
                        cat_evs.append(cev)
                    set_bank(("att", gp), cev)
            for i in range(NT):
                for ec in range(NCH):
                    res[("xb", ec, i)] = last_proj_mm
            cat_done = ev_max(*cat_evs)
            op_mm = None
            for i in range(NT if attn_level >= 6 else 0):
                for oc in range(NCH):
                    bnk = pc[0] % 2
                    pc[0] += 1
                    mm_ev = None
                    for hc in range(4):
                        mm_ev = S.E("tensor", lambda e, hc=hc, oc=oc, i=i, bnk=bnk,
                                    first=(hc == 0), last=(hc == 3): e.matmul(
                                        P[bnk][:, :],
                                        wo_sb[:, 1024 * hc + 128 * oc: 1024 * hc + 128 * (oc + 1)],
                                        cat[:, 2048 * hc + 512 * i: 2048 * hc + 512 * (i + 1)],
                                        start=first, stop=last),
                                    waits=ev_max(cat_done, wo_cast_ev, bank_ev(bnk)), inc="T")
                    op_mm = mm_ev
                    eev = S.E("vector", lambda e, oc=oc, i=i, bnk=bnk: e.scalar_tensor_tensor(
                        out=xs[:, 2048 * oc + 512 * i: 2048 * oc + 512 * (i + 1)],
                        in0=P[bnk][:, :], scalar=bo_sb[:, oc: oc + 1],
                        in1=xs[:, 2048 * oc + 512 * i: 2048 * oc + 512 * (i + 1)],
                        op0=ALU.add, op1=ALU.add),
                        waits=ev_max(mm_ev, bo_ld, xev.get(("x", oc, i)),
                                     res.get(("cast", oc, i))), inc="V")
                    set_bank(bnk, eev)
                    xev[("x", oc, i)] = eev
            if op_mm is not None:
                res["outproj_mm"] = op_mm

        def flush_store():
            allw = [(s, S.tot.get(s, 0)) for s in S.tot]
            for oc in range(NCH):
                S.E("sync", lambda e, oc=oc: e.dma_start(
                    out=Y[0, oc, :, :], in_=xs[:, 2048 * oc: 2048 * (oc + 1)]),
                    waits=allw, inc="OU", n=16)
                S.E("sync", lambda e, oc=oc: e.dma_start(
                    out=Y[1, oc, :, 0:1024], in_=xb.bitcast(F32)[:, 1024 * oc: 1024 * (oc + 1)]),
                    waits=allw, inc="OU", n=16)
            S.E("sync", lambda e: e.dma_start(
                out=Y[1, 0, :, 1024:1056], in_=stats_tc[:, :]), waits=allw, inc="OU", n=16)
            S.E("sync", lambda e: e.dma_start(
                out=Y[1, 0, :, 1056:1072], in_=mu_tc[:, :]), waits=allw, inc="OU", n=16)
            S.E("sync", lambda e: e.dma_start(
                out=Y[1, 0, :, 1072:1088], in_=inv_tc[:, :]), waits=allw, inc="OU", n=16)
            S.E("sync", lambda e: e.dma_start(
                out=Y[1, 0, :, 1088:1104], in_=var_tc[:, :]), waits=allw, inc="OU", n=16)
            S.E("sync", lambda e: e.dma_start(
                out=Y[1, 1, :, 1024:2048], in_=murep.bitcast(F32)[:, :]), waits=allw, inc="OU", n=16)
            S.E("sync", lambda e: e.dma_start(
                out=Y[1, 2, :, 1024:2048], in_=invrep.bitcast(F32)[:, :]), waits=allw, inc="OU", n=16)
            S.E("sync", lambda e: e.dma_start(
                out=Y[1, 3, 0:33, 1024:2048], in_=srowt[:, :]), waits=allw, inc="OU", n=16)
            S.E("sync", lambda e: e.dma_start(
                out=Y[1, 0, :, 1104:1120], in_=tmp_tc[:, :]), waits=allw, inc="OU", n=16)
            S.E("sync", lambda e: e.dma_start(
                out=Y[1, 4, :, 1024:2048], in_=qsb.bitcast(F32)[:, :]), waits=allw, inc="OU", n=16)
            S.E("sync", lambda e: e.dma_start(
                out=Y[1, 5, :, 1024:2048], in_=ksb.bitcast(F32)[:, :]), waits=allw, inc="OU", n=16)
            S.E("sync", lambda e: e.dma_start(
                out=Y[1, 6, :, 1024:2048], in_=vsb.bitcast(F32)[:, :]), waits=allw, inc="OU", n=16)
            S.E("sync", lambda e: e.dma_start(
                out=Y[1, 7, :, 1024:2048], in_=sc_sb[:, 0:1024]), waits=allw, inc="OU", n=16)
            S.E("sync", lambda e: e.dma_start(
                out=Y[1, 1, :, 0:512], in_=sm_sb.bitcast(F32)[:, :]), waits=allw, inc="OU", n=16)
            S.E("sync", lambda e: e.nop(), waits=[("OU", S.tot.get("OU", 0))])

        if upto == "storeb0":
            ln_phase(0, 0)
            ffn_phase(0)
            ln_phase(0, 1)
            attn_phase(0)
            ln_phase(0, 2)
            ffn_phase(0)
            for oc in range(NCH):
                w = ev_max(*[xev[("x", oc, i)] for i in range(NT)])
                S.E("sync", lambda e, oc=oc: e.dma_start(
                    out=Y[0, oc, :, :], in_=xs[:, 2048 * oc: 2048 * (oc + 1)]),
                    waits=w, inc="OU", n=16)
            S.E("sync", lambda e: e.nop(), waits=[("OU", S.tot.get("OU", 0))])
        elif upto.startswith("b1p"):
            nb1 = int(upto[3:])
            ln_phase(0, 0)
            ffn_phase(0)
            ln_phase(0, 1)
            attn_phase(0)
            ln_phase(0, 2)
            ffn_phase(0)
            for oc in range(NCH):
                w = ev_max(*[xev[("x", oc, i)] for i in range(NT)])
                res[("store", oc)] = S.E("sync", lambda e, oc=oc: e.dma_start(
                    out=Y[0, oc, :, :], in_=xs[:, 2048 * oc: 2048 * (oc + 1)]),
                    waits=w, inc="OU", n=16)
            all_st = ev_max(*[res[("store", oc)] for oc in range(NCH)])
            for i in range(NT):
                for ec in range(NCH):
                    xload_ev[(1, ec, i)] = xload(1, ec, i, waits=all_st)
                for ec in range(NCH):
                    xload_ev[(1, ec, i)] = xload_ev[(1, NCH - 1, i)]
            for i in range(NT):
                for ec in range(NCH):
                    xev.pop(("x", ec, i), None)
            ph1 = [lambda: ln_phase(1, 0), lambda: ffn_phase(1), lambda: ln_phase(1, 1),
                   lambda: attn_phase(1), lambda: ln_phase(1, 2), lambda: ffn_phase(1)]
            for fn in ph1[:nb1]:
                fn()
            allw = [(s, S.tot.get(s, 0)) for s in S.tot]
            for oc in range(NCH):
                S.E("sync", lambda e, oc=oc: e.dma_start(
                    out=Y[1, oc, :, :], in_=xs[:, 2048 * oc: 2048 * (oc + 1)]),
                    waits=allw, inc="OU", n=16)
            S.E("sync", lambda e: e.nop(), waits=[("OU", S.tot.get("OU", 0))])
        elif upto != "full":
            seq = {"load": [], "ln1": ["l0"], "ffn1": ["l0", "f"], "ln2": ["l0", "f", "l1"],
                   "attn": ["l0", "f", "l1", "a"], "ln3": ["l0", "f", "l1", "a", "l2"],
                   "ffn2": ["l0", "f", "l1", "a", "l2", "f2"]}[upto]
            for ph in seq:
                if ph == "l0":
                    ln_phase(0, 0)
                elif ph == "f":
                    ffn_phase(0)
                elif ph == "l1":
                    ln_phase(0, 1)
                elif ph == "a":
                    attn_phase(0)
                elif ph == "l2":
                    ln_phase(0, 2)
                elif ph == "f2":
                    ffn_phase(0)
            flush_store()
        else:
          for b in range(NB):
            ln_phase(b, 0)
            ffn_phase(b)
            ln_phase(b, 1)
            attn_phase(b)
            ln_phase(b, 2)
            ffn_phase(b)
            for oc in range(NCH):
                w = ev_max(*[xev[("x", oc, i)] for i in range(NT)])
                res[("store", oc)] = S.E("sync" if oc < 4 else "scalar",
                    lambda e, b=b, oc=oc: e.dma_start(
                        out=Y[b, oc, :, :], in_=xs[:, 2048 * oc: 2048 * (oc + 1)]),
                    waits=w, inc="OU", n=16, dma=True)
            if b + 1 < NB:
                all_st = ev_max(*[res[("store", oc)] for oc in range(NCH)])
                for i in range(NT):
                    for ec in range(NCH):
                        xload_ev[(b + 1, ec, i)] = xload(
                            b + 1, ec, i, waits=all_st,
                            q="sync" if ec < 4 else "scalar")
                    for ec in range(NCH):
                        xload_ev[(b + 1, ec, i)] = xload_ev[(b + 1, NCH - 1, i)]
                for i in range(NT):
                    for ec in range(NCH):
                        xev.pop(("x", ec, i), None)

        S.E("sync", lambda e: e.nop(), waits=[("OU", S.tot.get("OU", 0))])

        with nc.Block() as block:
            @block.tensor
            def _(e):
                S.run("tensor", e, sems)

            @block.vector
            def _(e):
                S.run("vector", e, sems)

            @block.scalar
            def _(e):
                S.run("scalar", e, sems)

            @block.gpsimd
            def _(e):
                S.run("gpsimd", e, sems)

            @block.sync
            def _(e):
                S.run("sync", e, sems)

    return nc


def _host_prep(inputs):
    x = np.asarray(inputs["L0_attn_x"], np.float32)
    Wk = np.asarray(inputs["L0_attn_Wk"], np.float32)
    Wq = np.asarray(inputs["L0_attn_Wq"], np.float32)
    Wv = np.asarray(inputs["L0_attn_Wv"], np.float32)
    W_out = np.asarray(inputs["L0_attn_W_out"], np.float32)
    b_out = np.asarray(inputs["L0_attn_b_out"], np.float32)
    g1r = np.asarray(inputs["L0_ffn_G1_R"], np.float32)
    g2r = np.asarray(inputs["L0_ffn_G2_R"], np.float32)
    g1i = np.asarray(inputs["L0_ffn_G1_I"], np.float32)
    g2i = np.asarray(inputs["L0_ffn_G2_I"], np.float32)
    br = np.asarray(inputs["L0_ffn_bias_real"], np.float32)
    bi = np.asarray(inputs["L0_ffn_bias_imag"], np.float32)

    com = {}
    com["WQ"] = np.ascontiguousarray(Wq.transpose(0, 2, 1))
    com["WK"] = np.ascontiguousarray(Wk.transpose(0, 2, 1))
    com["WV"] = np.ascontiguousarray(Wv.transpose(0, 2, 1))
    com["WO"] = np.ascontiguousarray(W_out.T)
    com["BO"] = np.ascontiguousarray(b_out.reshape(8, 128).T)
    com["BF"] = np.ascontiguousarray(np.concatenate([br, bi]).reshape(8, 128).T)
    com["G1T"] = np.ascontiguousarray(np.concatenate(
        [np.transpose(g1r[0], (1, 2, 0)), np.transpose(g1i[0], (1, 2, 0))], axis=0))
    com["G2T"] = np.ascontiguousarray(np.concatenate(
        [np.transpose(g2r[:, :, 0, :], (0, 2, 1)),
         np.transpose(g2i[:, :, 0, :], (0, 2, 1))], axis=0))
    com["SKA"] = np.kron(np.eye(32, dtype=np.float32), np.ones((1, 16), np.float32))
    com["SKB"] = np.tile(np.eye(16, dtype=np.float32), (1, 8))
    masks = np.empty((4, 128, 128), np.float32)
    for h, dil in enumerate(DILS):
        L = 16 // dil
        p = np.arange(128)
        jp, n = p // L, p % L
        jf, m = p // L, p % L
        allow = (jp[:, None] == jf[None, :]) & (m[None, :] <= n[:, None])
        masks[h] = np.where(allow, np.float32(0.0), np.float32(NEGINF))
    com["MASKS"] = masks
    com["IDENT"] = np.eye(128, dtype=np.float32)

    per_core = []
    for c in range(8):
        m = dict(com)
        m["X"] = np.ascontiguousarray(
            x[2 * c: 2 * c + 2].transpose(0, 2, 1).reshape(NB, 8, 128, T))
        per_core.append(m)
    return per_core


def kernel(**inputs):
    per_core = _host_prep(inputs)
    nc = build_program()
    res = run_bass_kernel_spmd(nc, per_core, core_ids=list(range(8)))
    outs = []
    for c in range(8):
        y = np.asarray(res.results[c]["Y"], np.float32)
        outs.append(y.reshape(NB, 1024, T).transpose(0, 2, 1))
    return np.ascontiguousarray(np.concatenate(outs, axis=0))


# version 07
# speedup vs baseline: 1.3382x; 1.1490x over previous
import contextlib
import numpy as np

import concourse.bass as bass
import concourse.mybir as mybir
from concourse.bass_utils import run_bass_kernel_spmd

F32 = mybir.dt.float32
BF16 = mybir.dt.bfloat16
AF = mybir.ActivationFunctionType
ALU = mybir.AluOpType
AX = mybir.AxisListType

DILS = [1, 2, 4, 8]
NORM = float(1.0 / np.sqrt(128.0))
NEGINF = -1.0e30
EPS = 1e-5
NB = 2
T = 2048
NT = 4
NCH = 8

ENGS = ("tensor", "vector", "scalar", "gpsimd", "sync")
ENG_SEM = {"tensor": "T", "vector": "V", "scalar": "S", "gpsimd": "G"}


class Sched:

    AUTO = ("vector", "scalar", "gpsimd")

    def __init__(self):
        self.q = {e: [] for e in ENGS}
        self.tot = {}
        self.waited = {e: {} for e in ENGS}
        self.scope = "startup"

    def E(self, eng, fn, waits=(), inc=None, n=1, dma=False):
        ws = []
        own = ENG_SEM.get(eng)
        auto = (eng in self.AUTO) and not dma
        if auto:
            prev = self.tot.get(own, 0)
            if prev > 0 and self.waited[eng].get(own, 0) < prev:
                self.waited[eng][own] = prev
                ws.append((own, prev))
            if inc is None:
                inc = own
        for (s, v) in waits:
            if v <= 0:
                continue
            if s == own:
                continue
            if self.waited[eng].get(s, 0) >= v:
                continue
            self.waited[eng][s] = v
            ws.append((s, v))
        if inc is not None:
            self.tot[inc] = self.tot.get(inc, 0) + n
        self.q[eng].append((ws, fn, inc, n, self.scope))
        return (inc, self.tot[inc]) if inc else None

    def run(self, eng, e, sems, nc=None):
        cur = None
        sid = None
        for (ws, fn, inc, n, scope) in self.q[eng]:
            if nc is not None and scope != cur:
                if cur is not None:
                    nc.leave_named_scope(cur, sid, False)
                sid, _ = nc.enter_named_scope(scope, False)
                cur = scope
            for (s, v) in ws:
                e.wait_ge(sems[s], v)
            ins = fn(e)
            if inc is not None:
                ins.then_inc(sems[inc], n)
        if nc is not None and cur is not None:
            nc.leave_named_scope(cur, sid, False)


def ev_max(*evs):
    out = {}
    for ev in evs:
        if not ev:
            continue
        if isinstance(ev, list):
            for (s, v) in ev:
                out[s] = max(out.get(s, 0), v)
        else:
            s, v = ev
            out[s] = max(out.get(s, 0), v)
    return list(out.items())


def build_program(upto="full", attn_level=6):
    nc = bass.Bass("TRN2", target_bir_lowering=False)
    S = Sched()

    X = nc.dram_tensor("X", [NB, 8, 128, T], F32, kind="ExternalInput")
    Y = nc.dram_tensor("Y", [NB, 8, 128, T], F32, kind="ExternalOutput")
    WQ = nc.dram_tensor("WQ", [4, 1024, 128], F32, kind="ExternalInput")
    WK = nc.dram_tensor("WK", [4, 1024, 128], F32, kind="ExternalInput")
    WV = nc.dram_tensor("WV", [4, 1024, 128], F32, kind="ExternalInput")
    WO = nc.dram_tensor("WO", [512, 1024], F32, kind="ExternalInput")
    BO = nc.dram_tensor("BO", [128, 8], F32, kind="ExternalInput")
    BF = nc.dram_tensor("BF", [128, 8], F32, kind="ExternalInput")
    G1T = nc.dram_tensor("G1T", [8, 32, 32], F32, kind="ExternalInput")
    G2T = nc.dram_tensor("G2T", [8, 16, 16], F32, kind="ExternalInput")
    SKA = nc.dram_tensor("SKA", [32, 512], F32, kind="ExternalInput")
    SKB = nc.dram_tensor("SKB", [16, 128], F32, kind="ExternalInput")
    MASKS = nc.dram_tensor("MASKS", [4, 128, 128], F32, kind="ExternalInput")
    IDENT = nc.dram_tensor("IDENT", [128, 128], F32, kind="ExternalInput")
    SROW = nc.dram_tensor("SROW", [4096], F32, kind="Internal")
    MIROW = nc.dram_tensor("MIROW", [6144], BF16, kind="Internal")

    with contextlib.ExitStack() as st:
        alloc = st.enter_context
        xs = alloc(nc.sbuf_tensor("xs", [128, 8 * T], F32))
        xb = alloc(nc.sbuf_tensor("xb", [128, 8 * T], BF16))
        invrep = alloc(nc.sbuf_tensor("invrep", [128, T], BF16))
        muiv_row = alloc(nc.sbuf_tensor("muiv_row", [1, T], BF16))
        sw_sb = alloc(nc.sbuf_tensor("sw_sb", [1, 2560], BF16))
        negones = alloc(nc.sbuf_tensor("negones", [128, 1], BF16))
        xbsq = alloc(nc.sbuf_tensor("xbsq", [128, 8 * 512], BF16))
        srowt = alloc(nc.sbuf_tensor("srowt", [33, 512], F32))
        stats_tc = alloc(nc.sbuf_tensor("stats_tc", [128, 32], F32))
        mu_tc = alloc(nc.sbuf_tensor("mu_tc", [128, 16], F32))
        tmp_tc = alloc(nc.sbuf_tensor("tmp_tc", [128, 16], F32))
        var_tc = alloc(nc.sbuf_tensor("var_tc", [128, 16], F32))
        inv_tc = alloc(nc.sbuf_tensor("inv_tc", [128, 16], F32))
        mi_bf = alloc(nc.sbuf_tensor("mi_bf", [128, 48], BF16))
        muiv_tc = alloc(nc.sbuf_tensor("muiv_tc", [128, 16], F32))
        qsb = alloc(nc.sbuf_tensor("qsb", [128, T], BF16))
        ksb = alloc(nc.sbuf_tensor("ksb", [128, T], BF16))
        vsb = alloc(nc.sbuf_tensor("vsb", [128, T], BF16))
        vt_sb = alloc(nc.sbuf_tensor("vt_sb", [128, 1024], BF16))
        sm_sb = alloc(nc.sbuf_tensor("sm_sb", [128, 1024], BF16))
        sc_sb = alloc(nc.sbuf_tensor("sc_sb", [128, 2048], F32))
        nmax = alloc(nc.sbuf_tensor("nmax", [128, 16], F32))
        rsum = alloc(nc.sbuf_tensor("rsum", [128, 16], F32))
        rrec = alloc(nc.sbuf_tensor("rrec", [128, 16], F32))
        cat = alloc(nc.sbuf_tensor("cat", [128, 4 * T], BF16))
        wqkv = alloc(nc.sbuf_tensor("wqkv", [128, 12 * 1024], BF16))
        wo_sb = alloc(nc.sbuf_tensor("wo_sb", [128, 4096], BF16))
        wr_sb = alloc(nc.sbuf_tensor("wr_sb", [128, 2048], BF16))
        wi_sb = alloc(nc.sbuf_tensor("wi_sb", [128, 2048], BF16))
        win_sb = alloc(nc.sbuf_tensor("win_sb", [128, 2048], BF16))
        mask_sb = alloc(nc.sbuf_tensor("mask_sb", [128, 512], F32))
        ident_bf = alloc(nc.sbuf_tensor("ident_bf", [128, 128], BF16))
        ones_bf = alloc(nc.sbuf_tensor("ones_bf", [128, 1], BF16))

        bo_sb = alloc(nc.sbuf_tensor("bo_sb", [128, 8], F32))
        bf_sb = alloc(nc.sbuf_tensor("bf_sb", [128, 8], F32))

        catf = cat.bitcast(F32)
        identf = sc_sb[:, 0:128]
        g1t_sb = qsb.bitcast(F32)[0:32, 0:256]
        g2t_sb = qsb.bitcast(F32)[0:16, 256:384]
        g1rep = ksb[0:32, 0:512]
        g2rep = ksb[0:16, 512:1024]
        ska_bf = vsb[0:32, 0:512]
        skb_bf = vsb[0:16, 512:640]
        wacc = [catf[:, 512:1024], catf[:, 1024:1536], catf[:, 3584:4096],
                sc_sb[:, 512:1024]]
        tmp_bf = sm_sb[:, 0:512]
        a_sb = sm_sb[:, 512:1024]
        b_sb = vt_sb[:, 0:512]
        ska_f = catf[0:32, 512:1024]
        skb_f = catf[0:16, 1024:1152]

        def qstage(p):
            return catf[:, 1536 + 1024 * p: 2560 + 1024 * p]

        P = [alloc(nc.psum_tensor(f"P{i}", [128, 512], F32)) for i in range(6)]
        P6 = alloc(nc.psum_tensor("P6", [128, 1024], BF16))
        P7 = alloc(nc.psum_tensor("P7", [128, 512], F32))

        sems = {n: alloc(nc.semaphore(n))
                for n in (["T", "V", "S", "G", "WS", "WQ0", "WQ1", "OU"]
                          + [f"XT{i}" for i in range(4)] + [f"ST{i}" for i in range(4)]
                          + [f"OU{i}" for i in range(8)])}

        res = {}

        def bank_ev(name):
            return res.get(("bank", name))

        def set_bank(name, ev):
            res[("bank", name)] = ev

        xload_ev = {}

        def xload(b, ec, i, waits=(), q="sync", sem=None):
            return S.E(q, lambda e, b=b, ec=ec, i=i: e.dma_start(
                out=xs[:, 2048 * ec + 512 * i: 2048 * ec + 512 * (i + 1)],
                in_=X[b, ec, :, 512 * i: 512 * (i + 1)]),
                waits=waits, inc=sem or f"XT{i}", n=16, dma=True)

        for ec in range(NCH):
            xload_ev[(0, ec, 0)] = xload(0, ec, 0)
        for ec in range(NCH):
            xload_ev[(0, ec, 0)] = xload_ev[(0, NCH - 1, 0)]
        g1_ld = S.E("sync", lambda e: e.dma_start(
            out=g1t_sb.rearrange("p (m b) -> p m b", b=32),
            in_=G1T.transpose([1, 0, 2])), inc="WS", n=16)
        g2_ld = S.E("sync", lambda e: e.dma_start(
            out=g2t_sb.rearrange("p (m ee) -> p m ee", ee=16),
            in_=G2T.transpose([1, 0, 2])), inc="WS", n=16)
        ska_ld = S.E("sync", lambda e: e.dma_start(out=ska_f, in_=SKA[:, :]), inc="WS", n=16)
        skb_ld = S.E("sync", lambda e: e.dma_start(out=skb_f, in_=SKB[:, :]), inc="WS", n=16)
        bo_ld = S.E("sync", lambda e: e.dma_start(out=bo_sb[:, :], in_=BO[:, :]), inc="WS", n=16)
        bf_ld = S.E("sync", lambda e: e.dma_start(out=bf_sb[:, :], in_=BF[:, :]), inc="WS", n=16)
        mask_ld = None
        for h in range(4):
            mask_ld = S.E("sync", lambda e, h=h: e.dma_start(
                out=mask_sb[:, 128 * h: 128 * (h + 1)], in_=MASKS[h, :, :]), inc="WS", n=16)
        id_ld = S.E("sync", lambda e: e.dma_start(out=identf[:, :], in_=IDENT[:, :]), inc="WS", n=16)
        for i in range(1, NT):
            for ec in range(NCH):
                xload_ev[(0, ec, i)] = xload(0, ec, i, q="scalar")
            for ec in range(NCH):
                xload_ev[(0, ec, i)] = xload_ev[(0, NCH - 1, i)]

        ones_ev = S.E("gpsimd", lambda e: e.memset(ones_bf[:, :], 1.0), inc="G")
        nones_ev = S.E("gpsimd", lambda e: e.memset(negones[:, :], -1.0), inc="G")
        W = {}
        qkv_cast_ev = {}

        def emit_weights():
            wo_cast_ev = None
            qstage_free = [None, None]
            par = 0
            specs = []
            for pi, WSRC in enumerate((WQ, WK, WV)):
                for h in range(4):
                    specs.append(((pi * 4 + h) * 1024, WSRC, h))
            for idx, (base, WSRC, h) in enumerate(specs):
                p = par % 2
                ld = S.E("sync", lambda e, WSRC=WSRC, h=h, p=p: e.dma_start(
                    out=qstage(p).rearrange("p (ec k) -> p ec k", k=128),
                    in_=WSRC[h].rearrange("(ec p) k -> p ec k", p=128)),
                    waits=ev_max(qstage_free[p]), inc=f"WQ{p}", n=16)
                eng = "vector" if idx % 2 == 0 else "gpsimd"
                cev = S.E(eng, lambda e, base=base, p=p: e.tensor_copy(
                    out=wqkv[:, base: base + 1024], in_=qstage(p)),
                    waits=ev_max(ld), inc=ENG_SEM[eng])
                qkv_cast_ev[base] = cev
                qstage_free[p] = cev
                par += 1
            for hc in range(4):
                p = par % 2
                ld = S.E("sync", lambda e, hc=hc, p=p: e.dma_start(
                    out=qstage(p), in_=WO[128 * hc: 128 * (hc + 1), :]),
                    waits=ev_max(qstage_free[p]), inc=f"WQ{p}", n=16)
                wo_cast_ev = S.E("vector", lambda e, hc=hc, p=p: e.tensor_copy(
                    out=wo_sb[:, 1024 * hc: 1024 * (hc + 1)], in_=qstage(p)),
                    waits=ev_max(ld), inc="V")
                qstage_free[p] = wo_cast_ev
                par += 1

            ska_ev = S.E("vector", lambda e: e.tensor_copy(out=ska_bf, in_=ska_f),
                         waits=ev_max(ska_ld), inc="V")
            skb_ev = S.E("vector", lambda e: e.tensor_copy(out=skb_bf, in_=skb_f),
                         waits=ev_max(skb_ld), inc="V")
            id_ev = S.E("gpsimd", lambda e: e.tensor_copy(out=ident_bf[:, :], in_=identf[:, :]),
                        waits=ev_max(id_ld), inc="G")
            bfs_ev = S.E("vector", lambda e: e.tensor_scalar_mul(
                out=bf_sb[:, :], in0=bf_sb[:, :], scalar1=0.5), waits=ev_max(bf_ld), inc="V")

            ffnw_evs = []
            rep_guard = {0: None, 1: None}
            ab_guard = {0: None, 1: None}
            wb = [0]
            for hf in range(2):
                prods = {ic: [] for ic in range(4)}
                for c in range(4):
                    m = hf * 4 + c
                    g1r = S.E("vector", lambda e, m=m: e.tensor_copy(
                        out=g1rep.rearrange("p (b ee) -> p b ee", ee=16),
                        in_=g1t_sb[:, 32 * m: 32 * (m + 1)].unsqueeze(2).to_broadcast([32, 32, 16])),
                        waits=ev_max(g1_ld, rep_guard[0]), inc="V")
                    g2r = S.E("vector", lambda e, m=m: e.tensor_copy(
                        out=g2rep.rearrange("p (b ee) -> p b ee", ee=16),
                        in_=g2t_sb[:, 16 * m: 16 * (m + 1)].unsqueeze(1).to_broadcast([16, 32, 16])),
                        waits=ev_max(g2_ld, rep_guard[1]), inc="V")
                    pb = 2 + (wb[0] % 2)
                    bmm = S.E("tensor", lambda e, pb=pb: e.matmul(
                        P[pb][:, :], skb_bf, g2rep, start=True, stop=True),
                        waits=ev_max(skb_ev, g2r, bank_ev(pb)), inc="T")
                    bev = S.E("scalar", lambda e, pb=pb: e.copy(out=b_sb, in_=P[pb][:, :]),
                              waits=ev_max(bmm, ab_guard[1]), inc="S")
                    set_bank(pb, bev)
                    rep_guard[1] = bmm
                    amm_last = None
                    aev = {}
                    for ic in range(4):
                        pa = wb[0] % 2
                        wb[0] += 1
                        amm = S.E("tensor", lambda e, pa=pa, ic=ic: e.matmul(
                            P[pa][:, :], ska_bf[:, 128 * ic: 128 * (ic + 1)], g1rep,
                            start=True, stop=True),
                            waits=ev_max(ska_ev, g1r, bank_ev(pa)), inc="T")
                        amm_last = amm
                        aev[ic] = S.E("scalar", lambda e, pa=pa: e.copy(out=a_sb, in_=P[pa][:, :]),
                                      waits=ev_max(amm, ab_guard[0]), inc="S")
                        set_bank(pa, aev[ic])
                        if c == 0:
                            pr = S.E("vector", lambda e, ic=ic: e.tensor_mul(
                                out=wacc[ic], in0=a_sb, in1=b_sb),
                                waits=ev_max(aev[ic], bev, res.get(("wacc", ic))), inc="V")
                        else:
                            S.E("vector", lambda e: e.tensor_mul(out=tmp_bf, in0=a_sb, in1=b_sb),
                                waits=ev_max(aev[ic], bev), inc="V")
                            pr = S.E("vector", lambda e, ic=ic: e.tensor_add(
                                out=wacc[ic], in0=wacc[ic], in1=tmp_bf), inc="V")
                        prods[ic].append(pr)
                        ab_guard[0] = pr
                    ab_guard[1] = prods[3][-1] if c > 0 else pr
                    rep_guard[0] = amm_last
                for ic in range(4):
                    dst = wr_sb if hf == 0 else wi_sb
                    cev = S.E("vector", lambda e, dst=dst, ic=ic: e.tensor_scalar_mul(
                        out=dst[:, 512 * ic: 512 * (ic + 1)], in0=wacc[ic], scalar1=0.5),
                        waits=ev_max(prods[ic][-1]), inc="V")
                    ffnw_evs.append(cev)
                    if hf == 1:
                        cev = S.E("vector", lambda e, ic=ic: e.tensor_scalar_mul(
                            out=win_sb[:, 512 * ic: 512 * (ic + 1)], in0=wacc[ic], scalar1=-0.5),
                            inc="V")
                        ffnw_evs.append(cev)
                    res[("wacc", ic)] = cev
            wcast_ev = ev_max(*ffnw_evs)
            sw_evs = []
            swb = [0]
            for oc in range(NCH):
                bnk = swb[0] % 2
                swb[0] += 1
                mm = None
                for ic in range(NCH):
                    if oc < 4:
                        stat = (wr_sb[:, 512 * ic + 128 * oc: 512 * ic + 128 * (oc + 1)]
                                if ic < 4 else
                                win_sb[:, 512 * (ic - 4) + 128 * oc: 512 * (ic - 4) + 128 * (oc + 1)])
                    else:
                        stat = (wi_sb[:, 512 * ic + 128 * (oc - 4): 512 * ic + 128 * (oc - 3)]
                                if ic < 4 else
                                wr_sb[:, 512 * (ic - 4) + 128 * (oc - 4): 512 * (ic - 4) + 128 * (oc - 3)])
                    mm = S.E("tensor", lambda e, stat=stat, bnk=bnk, first=(ic == 0), last=(ic == 7):
                             e.matmul(P[bnk][0:1, 0:128], negones[:, :], stat,
                                      start=first, stop=last),
                             waits=ev_max(nones_ev, wcast_ev, bank_ev(bnk)), inc="T")
                ev = S.E("scalar", lambda e, oc=oc, bnk=bnk: e.copy(
                    out=sw_sb[0:1, 128 * oc: 128 * (oc + 1)], in_=P[bnk][0:1, 0:128]),
                    waits=ev_max(mm), inc="S")
                set_bank(bnk, ev)
                sw_evs.append(ev)
            for pi in range(3):
                for h in range(4):
                    base = (pi * 4 + h) * 1024
                    bnk = swb[0] % 2
                    swb[0] += 1
                    mm = None
                    for ec in range(NCH):
                        mm = S.E("tensor", lambda e, base=base, ec=ec, bnk=bnk,
                                 first=(ec == 0), last=(ec == 7): e.matmul(
                                     P[bnk][0:1, 0:128], negones[:, :],
                                     wqkv[:, base + 128 * ec: base + 128 * (ec + 1)],
                                     start=first, stop=last),
                                 waits=ev_max(nones_ev, qkv_cast_ev[base], bank_ev(bnk)), inc="T")
                    ev = S.E("scalar", lambda e, pi=pi, h=h, bnk=bnk: e.copy(
                        out=sw_sb[0:1, 1024 + (pi * 4 + h) * 128: 1024 + (pi * 4 + h) * 128 + 128],
                        in_=P[bnk][0:1, 0:128]),
                        waits=ev_max(mm), inc="S")
                    set_bank(bnk, ev)
                    sw_evs.append(ev)
            sw_ev = ev_max(*sw_evs)
            catfree = ev_max(qstage_free[0], qstage_free[1], wcast_ev)
            ffnw_ev = wcast_ev
            res[("vts", 0)] = wcast_ev
            res["qkvfree"] = wcast_ev
            res[("sm", 0)] = wcast_ev
            res[("sm", 1)] = wcast_ev
            res[("sc", 0)] = id_ev
            res[("sc", 1)] = wcast_ev

            W["ffnw"] = ffnw_ev
            W["sw"] = sw_ev
            W["catfree"] = catfree
            W["wo"] = wo_cast_ev
            W["bfs"] = bfs_ev
            W["bo"] = bo_ld
            W["id"] = id_ev
            W["mask"] = mask_ld

        xev = {}

        def ln_phase(b, ln_idx):
            cast_ev = {}
            for ec in range(NCH):
                eng = "vector" if ec % 2 == 0 else "scalar"
                w = []
                for i in range(NT):
                    if ln_idx == 0:
                        w += ev_max(xload_ev.get((b, ec, i)))
                    else:
                        w += ev_max(xev.get(("x", ec, i)))
                    w += ev_max(res.get(("xb", ec, i)))
                w = ev_max(w)
                if eng == "vector":
                    cast_ev[ec] = S.E(eng, lambda e, ec=ec: e.tensor_copy(
                        out=xb[:, 2048 * ec: 2048 * (ec + 1)],
                        in_=xs[:, 2048 * ec: 2048 * (ec + 1)]),
                        waits=w, inc=ENG_SEM[eng])
                else:
                    cast_ev[ec] = S.E(eng, lambda e, ec=ec: e.copy(
                        out=xb[:, 2048 * ec: 2048 * (ec + 1)],
                        in_=xs[:, 2048 * ec: 2048 * (ec + 1)]),
                        waits=w, inc=ENG_SEM[eng])
                for i in range(NT):
                    res[("cast", ec, i)] = cast_ev[ec]
            allcast = ev_max(*cast_ev.values())
            fi_evs = []
            mm_last = None
            for i in range(NT):
                sq = S.E("scalar", lambda e, i=i: e.activation(
                    out=xbsq[:, :],
                    in_=bass.AP(xb, 512 * i, [[16384, 128], [2048, 8], [1, 512]]),
                    func=AF.Square),
                    waits=ev_max(allcast, res.get("xbsq")), inc="S")
                bnk = 2 + (i % 2)
                mm_ev = None
                for ec in range(NCH):
                    mm_ev = S.E("tensor", lambda e, ec=ec, i=i, bnk=bnk: e.matmul(
                        P[bnk][0:1, :], ones_bf[:, :],
                        xb[:, 2048 * ec + 512 * i: 2048 * ec + 512 * (i + 1)],
                        start=(ec == 0), stop=(ec == 7)),
                        waits=ev_max(cast_ev[ec], ones_ev, bank_ev(bnk)), inc="T")
                for ec in range(NCH):
                    mm_ev = S.E("tensor", lambda e, ec=ec, bnk=bnk: e.matmul(
                        P[bnk][32:33, :], ones_bf[:, :], xbsq[:, 512 * ec: 512 * (ec + 1)],
                        start=(ec == 0), stop=(ec == 7)),
                        waits=ev_max(sq), inc="T")
                res["xbsq"] = mm_ev
                mm_last = mm_ev
                slot = 0
                S.E("scalar", lambda e, bnk=bnk, slot=slot: e.copy(
                    out=srowt[0:1, slot: slot + 512], in_=P[bnk][0:1, :]),
                    waits=ev_max(mm_ev, res.get(("srowt", i % 2))), inc="S")
                sevac = S.E("scalar", lambda e, bnk=bnk, slot=slot: e.copy(
                    out=srowt[32:33, slot: slot + 512], in_=P[bnk][32:33, :]), inc="S")
                set_bank(bnk, sevac)
                S.E("gpsimd", lambda e, i=i, slot=slot: e.dma_start(
                    out=SROW[512 * i: 512 * (i + 1)], in_=srowt[0:1, slot: slot + 512]),
                    waits=ev_max(sevac, res.get(("statchain", i))), inc=f"ST{i}", n=16, dma=True)
                fo = S.E("gpsimd", lambda e, i=i, slot=slot: e.dma_start(
                    out=SROW[2048 + 512 * i: 2048 + 512 * (i + 1)],
                    in_=srowt[32:33, slot: slot + 512]), inc=f"ST{i}", n=16, dma=True)
                res[("srowt", i % 2)] = fo
                S.E("gpsimd", lambda e, i=i: e.dma_start(
                    out=stats_tc[:, 4 * i: 4 * (i + 1)],
                    in_=SROW[512 * i: 512 * (i + 1)].rearrange("(p c) -> p c", c=4)),
                    waits=ev_max(fo), inc=f"ST{i}", n=16, dma=True)
                fi = S.E("gpsimd", lambda e, i=i: e.dma_start(
                    out=stats_tc[:, 16 + 4 * i: 16 + 4 * (i + 1)],
                    in_=SROW[2048 + 512 * i: 2048 + 512 * (i + 1)].rearrange("(p c) -> p c", c=4)),
                    inc=f"ST{i}", n=16, dma=True)
                fi_evs.append(fi)
            S.E("vector", lambda e: e.tensor_scalar_mul(
                out=mu_tc[:, :], in0=stats_tc[:, 0:16], scalar1=1.0 / 1024.0),
                waits=ev_max(*fi_evs), inc="V")
            S.E("vector", lambda e: e.tensor_mul(
                out=tmp_tc[:, :], in0=mu_tc[:, :], in1=mu_tc[:, :]), inc="V")
            S.E("vector", lambda e: e.tensor_scalar_mul(
                out=var_tc[:, :], in0=stats_tc[:, 16:32], scalar1=1.0 / 1024.0), inc="V")
            S.E("vector", lambda e: e.tensor_sub(
                out=var_tc[:, :], in0=var_tc[:, :], in1=tmp_tc[:, :]), inc="V")
            S.E("vector", lambda e: e.tensor_scalar_add(
                out=var_tc[:, :], in0=var_tc[:, :], scalar1=EPS), inc="V")
            vrec = S.E("vector", lambda e: e.reciprocal(
                out=tmp_tc[:, :], in_=var_tc[:, :]), inc="V")
            ssq = S.E("scalar", lambda e: e.activation(
                out=inv_tc[:, :], in_=tmp_tc[:, :], func=AF.Sqrt),
                waits=ev_max(vrec), inc="S")
            S.E("vector", lambda e: e.tensor_mul(
                out=muiv_tc[:, :], in0=mu_tc[:, :], in1=inv_tc[:, :]),
                waits=ev_max(ssq), inc="V")
            S.E("vector", lambda e: e.tensor_copy(
                out=mi_bf[:, 16:32], in_=inv_tc[:, :]), inc="V")
            mic = S.E("vector", lambda e: e.tensor_copy(
                out=mi_bf[:, 32:48], in_=muiv_tc[:, :]), inc="V")
            S.E("gpsimd", lambda e: e.dma_start(
                out=bass.AP(MIROW, 2048, [[4, 128], [512, 4], [1, 4]]),
                in_=mi_bf[:, 16:32].rearrange("p (i c) -> p i c", c=4)),
                waits=ev_max(mic), inc="ST0", n=16, dma=True)
            uo = S.E("gpsimd", lambda e: e.dma_start(
                out=bass.AP(MIROW, 4096, [[4, 128], [512, 4], [1, 4]]),
                in_=mi_bf[:, 32:48].rearrange("p (i c) -> p i c", c=4)),
                inc="ST0", n=16, dma=True)
            rep = S.E("gpsimd", lambda e: e.dma_start(
                out=invrep[:, :],
                in_=MIROW[2048:4096].unsqueeze(0).to_broadcast([128, 2048])),
                waits=ev_max(uo, res.get("repl")), inc="ST0", n=16, dma=True)
            muiv = S.E("gpsimd", lambda e: e.dma_start(
                out=muiv_row[0:1, :], in_=MIROW[4096:6144].unsqueeze(0)),
                waits=ev_max(res.get("muiv")), inc="ST0", n=16, dma=True)
            res["muivrow"] = muiv
            for i in range(NT):
                res[("statchain", i)] = muiv
            mul_evs = []
            for ec in range(NCH):
                eng = "gpsimd" if ec >= 5 else "vector"
                mev = S.E(eng, lambda e, ec=ec: e.tensor_mul(
                    out=xb[:, 2048 * ec: 2048 * (ec + 1)],
                    in0=xs[:, 2048 * ec: 2048 * (ec + 1)], in1=invrep[:, :]),
                    waits=ev_max(rep, mm_last), inc=ENG_SEM[eng])
                for i in range(NT):
                    res[("xln", ec, i)] = mev
                mul_evs.append(mev)
            res["repl"] = ev_max(*mul_evs)

        pc = [0]

        POOL = (0, 1, 4, 5)

        def ffn_phase(b):
            for oc in range(NCH):
                for i in range(NT):
                    bnk = POOL[pc[0] % 4]
                    pc[0] += 1
                    mm_ev = None
                    for ic in range(NCH):
                        if oc < 4:
                            stat = (wr_sb[:, 512 * ic + 128 * oc: 512 * ic + 128 * (oc + 1)]
                                    if ic < 4 else
                                    win_sb[:, 512 * (ic - 4) + 128 * oc: 512 * (ic - 4) + 128 * (oc + 1)])
                        else:
                            stat = (wi_sb[:, 512 * ic + 128 * (oc - 4): 512 * ic + 128 * (oc - 3)]
                                    if ic < 4 else
                                    wr_sb[:, 512 * (ic - 4) + 128 * (oc - 4): 512 * (ic - 4) + 128 * (oc - 3)])
                        w = ev_max(res.get(("xln", ic, i)), bank_ev(bnk), W.get("ffnw"))
                        mm_ev = S.E("tensor", lambda e, stat=stat, ic=ic, i=i, bnk=bnk,
                                    first=(ic == 0): e.matmul(
                                        P[bnk][:, :], stat,
                                        xb[:, 2048 * ic + 512 * i: 2048 * ic + 512 * (i + 1)],
                                        start=first, stop=False),
                                    waits=w, inc="T")
                        res[("xb", ic, i)] = mm_ev
                    mm_ev = S.E("tensor", lambda e, oc=oc, i=i, bnk=bnk: e.matmul(
                        P[bnk][:, :], sw_sb[0:1, 128 * oc: 128 * (oc + 1)],
                        muiv_row[0:1, 512 * i: 512 * (i + 1)],
                        start=False, stop=True),
                        waits=ev_max(W.get("sw"), res.get("muivrow")), inc="T")
                    res["muiv"] = mm_ev
                    eev = S.E("vector", lambda e, oc=oc, i=i, bnk=bnk: e.scalar_tensor_tensor(
                        out=xs[:, 2048 * oc + 512 * i: 2048 * oc + 512 * (i + 1)],
                        in0=P[bnk][:, :], scalar=bf_sb[:, oc: oc + 1],
                        in1=xs[:, 2048 * oc + 512 * i: 2048 * oc + 512 * (i + 1)],
                        op0=ALU.add, op1=ALU.add),
                        waits=ev_max(mm_ev, W.get("bfs"), xev.get(("x", oc, i)),
                                     res.get(("cast", oc, i))), inc="V")
                    set_bank(bnk, eev)
                    xev[("x", oc, i)] = eev

        def attn_phase(b):
            catz_w = ev_max(W.get("catfree"), res.get("outproj_mm"))
            catz = [S.E("gpsimd", lambda e, h=h: e.memset(
                cat[:, 2048 * h: 2048 * (h + 1)], 0.0), waits=catz_w, inc="G")
                for h in range(4)]
            catz_ev = ev_max(*catz)
            ln_done = [("V", S.tot.get("V", 0)), ("G", S.tot.get("G", 0))]
            cat_evs = []
            last_proj_mm = None
            for h, dil in enumerate(DILS):
                L = 16 // dil
                J = 128 // L
                cols_h = 128 * L
                nsub = max(1, cols_h // 512)
                qkv_done = []
                for pi, dst in enumerate((qsb, ksb, vsb)):
                    base = (pi * 4 + h) * 1024
                    for j in range(nsub):
                        ncols = min(512, cols_h)
                        nl = ncols // 128
                        bnk = POOL[pc[0] % 4]
                        pc[0] += 1
                        mm_ev = None
                        for ec in range(NCH):
                            xv = xb[:, 2048 * ec: 2048 * (ec + 1)].rearrange(
                                "p (l s) -> p l s", s=128)[:, 4 * j * dil: (4 * j + nl) * dil: dil, :]
                            w = ev_max(qkv_cast_ev[base], bank_ev(bnk), ln_done)
                            mm_ev = S.E("tensor", lambda e, base=base, ec=ec, xv=xv, bnk=bnk,
                                        ncols=ncols, first=(ec == 0): e.matmul(
                                            P[bnk][:, 0:ncols],
                                            wqkv[:, base + 128 * ec: base + 128 * (ec + 1)],
                                            xv, start=first, stop=False),
                                        waits=w, inc="T")
                        mv = muiv_row[0:1, :].rearrange("p (l s) -> p l s", s=128)[
                            :, 4 * j * dil: (4 * j + nl) * dil: dil, :]
                        mm_ev = S.E("tensor", lambda e, base=base, mv=mv, bnk=bnk, ncols=ncols,
                                    pi=pi, h=h: e.matmul(
                                        P[bnk][:, 0:ncols],
                                        sw_sb[0:1, 1024 + (pi * 4 + h) * 128: 1024 + (pi * 4 + h) * 128 + 128],
                                        mv, start=False, stop=True),
                                    waits=ev_max(W.get("sw"), res.get("muivrow")), inc="T")
                        res["muiv"] = mm_ev
                        last_proj_mm = mm_ev
                        dsv = dst[:, 0:cols_h].rearrange(
                            "p (k jj l) -> p k jj l", k=L, jj=J, l=L
                        ).transpose([0, 3, 1, 2])[:, 4 * j: 4 * j + nl, :, :]
                        psv = P[bnk][:, 0:ncols].rearrange(
                            "p (l k jj) -> p l k jj", k=L, jj=J)
                        eev = S.E("scalar", lambda e, dsv=dsv, psv=psv: e.copy(
                            out=dsv, in_=psv),
                            waits=ev_max(mm_ev, res.get("qkvfree")), inc="S")
                        set_bank(bnk, eev)
                        qkv_done.append(eev)
                qkv_ev = ev_max(*qkv_done)

                ngrp = (L + 3) // 4 if attn_level >= 2 else 0
                for g in range(ngrp):
                    ks = list(range(4 * g, min(4 * g + 4, L)))
                    nk = len(ks)
                    gp = g % 2
                    sc_bnk = 4 + gp
                    slot = 512 * gp
                    kaps = {k: ksb[:, 128 * k: 128 * (k + 1)] for k in ks}
                    qaps = {k: qsb[:, 128 * k: 128 * (k + 1)] for k in ks}
                    vaps = {k: vsb[:, 128 * k: 128 * (k + 1)] for k in ks}
                    smm = None
                    for kk, k in enumerate(ks):
                        smm = S.E("tensor", lambda e, kap=kaps[k], qap=qaps[k], kk=kk,
                                  sc_bnk=sc_bnk: e.matmul(
                                      P[sc_bnk][:, 128 * kk: 128 * (kk + 1)], kap, qap,
                                      start=True, stop=True),
                                  waits=ev_max(qkv_ev, bank_ev(sc_bnk)), inc="T")
                    vtb = P6 if gp == 0 else P[2].bitcast(BF16)
                    vtk = 6 if gp == 0 else 2
                    vmm = {}
                    if attn_level >= 3:
                        for kk, k in enumerate(ks):
                            vmm[k] = S.E("tensor", lambda e, vap=vaps[k], kk=kk, vtb=vtb: e.transpose(
                                vtb[:, 128 * kk: 128 * (kk + 1)], vap, ident_bf[:, :]),
                                waits=ev_max(W.get("id"), bank_ev(vtk), qkv_ev), inc="T")
                    mrep = mask_sb[:, 128 * h: 128 * (h + 1)].unsqueeze(1).to_broadcast([128, nk, 128])
                    scm = S.E("vector", lambda e, slot=slot, nk=nk, sc_bnk=sc_bnk, mrep=mrep:
                              e.scalar_tensor_tensor(
                                  out=sc_sb[:, slot: slot + 128 * nk],
                                  in0=P[sc_bnk][:, 0:128 * nk], scalar=NORM, in1=mrep,
                                  op0=ALU.mult, op1=ALU.add),
                              waits=ev_max(smm, res.get(("sc", gp)), W.get("mask")), inc="V")
                    set_bank(sc_bnk, scm)
                    rmx = S.E("vector", lambda e, slot=slot, nk=nk, gp=gp: e.reduce_max(
                        out=nmax[:, 4 * gp: 4 * gp + nk],
                        in_=sc_sb[:, slot: slot + 128 * nk].rearrange("p (k f) -> p k f", f=128),
                        axis=AX.X, negate=True),
                        waits=ev_max(res.get(("nmax", gp))), inc="V")
                    exp_evs = []
                    for kk, k in enumerate(ks):
                        exp_evs.append(S.E("scalar", lambda e, slot=slot, kk=kk, gp=gp: e.activation(
                            out=sc_sb[:, 1024 + slot + 128 * kk: 1024 + slot + 128 * (kk + 1)],
                            in_=sc_sb[:, slot + 128 * kk: slot + 128 * (kk + 1)],
                            func=AF.Exp, bias=nmax[:, 4 * gp + kk: 4 * gp + kk + 1], scale=1.0,
                            accum_out=rsum[:, 4 * gp + kk: 4 * gp + kk + 1]),
                            waits=ev_max(rmx, res.get(("exp", gp)), res.get(("rsum", gp))), inc="S"))
                    res[("sc", gp)] = exp_evs[-1]
                    res[("nmax", gp)] = exp_evs[-1]
                    rc = S.E("vector", lambda e, gp=gp, nk=nk: e.reciprocal(
                        out=rrec[:, 4 * gp: 4 * gp + nk], in_=rsum[:, 4 * gp: 4 * gp + nk]),
                        waits=ev_max(*exp_evs, res.get(("rrec", gp))), inc="V")
                    res[("rsum", gp)] = rc
                    sm_evs = {}
                    for kk, k in enumerate(ks):
                        sm_evs[k] = S.E("vector", lambda e, slot=slot, kk=kk, gp=gp:
                                        e.tensor_scalar_mul(
                                            out=sm_sb[:, slot + 128 * kk: slot + 128 * (kk + 1)],
                                            in0=sc_sb[:, 1024 + slot + 128 * kk: 1024 + slot + 128 * (kk + 1)],
                                            scalar1=rrec[:, 4 * gp + kk: 4 * gp + kk + 1]),
                                        waits=ev_max(rc, res.get(("sm", gp))), inc="V")
                    res[("exp", gp)] = ev_max(*sm_evs.values())
                    vte = {}
                    if attn_level >= 3:
                        gv = S.E("scalar", lambda e, slot=slot, vtb=vtb, nk=nk: e.copy(
                            out=vt_sb[:, slot: slot + 128 * nk],
                            in_=vtb[:, 0: 128 * nk]),
                            waits=ev_max(vmm[ks[-1]], res.get(("vts", gp))), inc="S")
                        for k in ks:
                            vte[k] = gv
                        set_bank(vtk, gv)
                    attb = P7 if gp == 0 else P[3]
                    atk = 7 if gp == 0 else 3
                    amm = {}
                    if attn_level < 4:
                        continue
                    for kk, k in enumerate(ks):
                        amm[k] = S.E("tensor", lambda e, slot=slot, kk=kk, attb=attb: e.matmul(
                            attb[:, 128 * kk: 128 * (kk + 1)],
                            vt_sb[:, slot + 128 * kk: slot + 128 * (kk + 1)],
                            sm_sb[:, slot + 128 * kk: slot + 128 * (kk + 1)],
                            start=True, stop=True),
                            waits=ev_max(sm_evs[k], vte[k], bank_ev(atk)), inc="T")
                    res[("vts", gp)] = amm[ks[-1]]
                    res[("sm", gp)] = amm[ks[-1]]
                    res["qkvfree"] = amm[ks[-1]]
                    cev = None
                    for kk, k in enumerate(ks):
                        if attn_level >= 5:
                            catv = cat[:, 2048 * h: 2048 * (h + 1)].rearrange(
                                "p (c r) -> p c r", r=dil)[:, 128 * k: 128 * (k + 1), 0]
                        else:
                            catv = cat[:, 2048 * h + 128 * k: 2048 * h + 128 * (k + 1)]
                        cev = S.E("vector", lambda e, catv=catv, kk=kk, attb=attb: e.tensor_copy(
                            out=catv, in_=attb[:, 128 * kk: 128 * (kk + 1)]),
                            waits=ev_max(amm[ks[-1]], catz_ev), inc="V")
                        cat_evs.append(cev)
                    set_bank(atk, cev)
            for i in range(NT):
                for ec in range(NCH):
                    res[("xb", ec, i)] = last_proj_mm
            cat_done = ev_max(*cat_evs)
            op_mm = None
            for i in range(NT if attn_level >= 6 else 0):
                for oc in range(NCH):
                    bnk = POOL[pc[0] % 4]
                    pc[0] += 1
                    mm_ev = None
                    for hc in range(4):
                        mm_ev = S.E("tensor", lambda e, hc=hc, oc=oc, i=i, bnk=bnk,
                                    first=(hc == 0), last=(hc == 3): e.matmul(
                                        P[bnk][:, :],
                                        wo_sb[:, 1024 * hc + 128 * oc: 1024 * hc + 128 * (oc + 1)],
                                        cat[:, 2048 * hc + 512 * i: 2048 * hc + 512 * (i + 1)],
                                        start=first, stop=last),
                                    waits=ev_max(cat_done, W.get("wo"), bank_ev(bnk)), inc="T")
                    op_mm = mm_ev
                    eev = S.E("vector", lambda e, oc=oc, i=i, bnk=bnk: e.scalar_tensor_tensor(
                        out=xs[:, 2048 * oc + 512 * i: 2048 * oc + 512 * (i + 1)],
                        in0=P[bnk][:, :], scalar=bo_sb[:, oc: oc + 1],
                        in1=xs[:, 2048 * oc + 512 * i: 2048 * oc + 512 * (i + 1)],
                        op0=ALU.add, op1=ALU.add),
                        waits=ev_max(mm_ev, W.get("bo"), xev.get(("x", oc, i)),
                                     res.get(("cast", oc, i))), inc="V")
                    set_bank(bnk, eev)
                    xev[("x", oc, i)] = eev
            if op_mm is not None:
                res["outproj_mm"] = op_mm

        def flush_store():
            allw = [(s, S.tot.get(s, 0)) for s in S.tot]
            for oc in range(NCH):
                S.E("sync", lambda e, oc=oc: e.dma_start(
                    out=Y[0, oc, :, :], in_=xs[:, 2048 * oc: 2048 * (oc + 1)]),
                    waits=allw, inc="OU", n=16)
                S.E("sync", lambda e, oc=oc: e.dma_start(
                    out=Y[1, oc, :, 0:1024], in_=xb.bitcast(F32)[:, 1024 * oc: 1024 * (oc + 1)]),
                    waits=allw, inc="OU", n=16)
            S.E("sync", lambda e: e.dma_start(
                out=Y[1, 0, :, 1024:1056], in_=stats_tc[:, :]), waits=allw, inc="OU", n=16)
            S.E("sync", lambda e: e.dma_start(
                out=Y[1, 0, :, 1056:1072], in_=mu_tc[:, :]), waits=allw, inc="OU", n=16)
            S.E("sync", lambda e: e.dma_start(
                out=Y[1, 0, :, 1072:1088], in_=inv_tc[:, :]), waits=allw, inc="OU", n=16)
            S.E("sync", lambda e: e.dma_start(
                out=Y[1, 0, :, 1088:1104], in_=var_tc[:, :]), waits=allw, inc="OU", n=16)
            S.E("sync", lambda e: e.dma_start(
                out=Y[1, 1, :, 1024:2048], in_=murep.bitcast(F32)[:, :]), waits=allw, inc="OU", n=16)
            S.E("sync", lambda e: e.dma_start(
                out=Y[1, 2, :, 1024:2048], in_=invrep.bitcast(F32)[:, :]), waits=allw, inc="OU", n=16)
            S.E("sync", lambda e: e.dma_start(
                out=Y[1, 3, 0:33, 1024:2048], in_=srowt[:, :]), waits=allw, inc="OU", n=16)
            S.E("sync", lambda e: e.dma_start(
                out=Y[1, 0, :, 1104:1120], in_=tmp_tc[:, :]), waits=allw, inc="OU", n=16)
            S.E("sync", lambda e: e.dma_start(
                out=Y[1, 4, :, 1024:2048], in_=qsb.bitcast(F32)[:, :]), waits=allw, inc="OU", n=16)
            S.E("sync", lambda e: e.dma_start(
                out=Y[1, 5, :, 1024:2048], in_=ksb.bitcast(F32)[:, :]), waits=allw, inc="OU", n=16)
            S.E("sync", lambda e: e.dma_start(
                out=Y[1, 6, :, 1024:2048], in_=vsb.bitcast(F32)[:, :]), waits=allw, inc="OU", n=16)
            S.E("sync", lambda e: e.dma_start(
                out=Y[1, 7, :, 1024:2048], in_=sc_sb[:, 0:1024]), waits=allw, inc="OU", n=16)
            S.E("sync", lambda e: e.dma_start(
                out=Y[1, 1, :, 0:512], in_=sm_sb.bitcast(F32)[:, :]), waits=allw, inc="OU", n=16)
            S.E("sync", lambda e: e.nop(), waits=[("OU", S.tot.get("OU", 0))])

        if upto == "storeb0":
            ln_phase(0, 0)
            emit_weights()
            ffn_phase(0)
            ln_phase(0, 1)
            attn_phase(0)
            ln_phase(0, 2)
            ffn_phase(0)
            for oc in range(NCH):
                w = ev_max(*[xev[("x", oc, i)] for i in range(NT)])
                S.E("sync", lambda e, oc=oc: e.dma_start(
                    out=Y[0, oc, :, :], in_=xs[:, 2048 * oc: 2048 * (oc + 1)]),
                    waits=w, inc="OU", n=16)
            S.E("sync", lambda e: e.nop(), waits=[("OU", S.tot.get("OU", 0))])
        elif upto.startswith("b1p"):
            nb1 = int(upto[3:])
            ln_phase(0, 0)
            emit_weights()
            ffn_phase(0)
            ln_phase(0, 1)
            attn_phase(0)
            ln_phase(0, 2)
            ffn_phase(0)
            for oc in range(NCH):
                w = ev_max(*[xev[("x", oc, i)] for i in range(NT)])
                res[("store", oc)] = S.E("sync", lambda e, oc=oc: e.dma_start(
                    out=Y[0, oc, :, :], in_=xs[:, 2048 * oc: 2048 * (oc + 1)]),
                    waits=w, inc="OU", n=16)
            all_st = ev_max(*[res[("store", oc)] for oc in range(NCH)])
            for i in range(NT):
                for ec in range(NCH):
                    xload_ev[(1, ec, i)] = xload(1, ec, i, waits=all_st)
                for ec in range(NCH):
                    xload_ev[(1, ec, i)] = xload_ev[(1, NCH - 1, i)]
            for i in range(NT):
                for ec in range(NCH):
                    xev.pop(("x", ec, i), None)
            ph1 = [lambda: ln_phase(1, 0), lambda: ffn_phase(1), lambda: ln_phase(1, 1),
                   lambda: attn_phase(1), lambda: ln_phase(1, 2), lambda: ffn_phase(1)]
            for fn in ph1[:nb1]:
                fn()
            allw = [(s, S.tot.get(s, 0)) for s in S.tot]
            for oc in range(NCH):
                S.E("sync", lambda e, oc=oc: e.dma_start(
                    out=Y[1, oc, :, :], in_=xs[:, 2048 * oc: 2048 * (oc + 1)]),
                    waits=allw, inc="OU", n=16)
            S.E("sync", lambda e: e.nop(), waits=[("OU", S.tot.get("OU", 0))])
        elif upto != "full":
            seq = {"load": [], "ln1": ["l0"], "ffn1": ["l0", "f"], "ln2": ["l0", "f", "l1"],
                   "attn": ["l0", "f", "l1", "a"], "ln3": ["l0", "f", "l1", "a", "l2"],
                   "ffn2": ["l0", "f", "l1", "a", "l2", "f2"]}[upto]
            for ph in seq:
                if ph == "l0":
                    ln_phase(0, 0)
                    emit_weights()
                elif ph == "f":
                    ffn_phase(0)
                elif ph == "l1":
                    ln_phase(0, 1)
                elif ph == "a":
                    attn_phase(0)
                elif ph == "l2":
                    ln_phase(0, 2)
                elif ph == "f2":
                    ffn_phase(0)
            flush_store()
        else:
          for b in range(NB):
            S.scope = f"b{b}_ln1"; ln_phase(b, 0)
            if b == 0:
                S.scope = "weights"; emit_weights()
            S.scope = f"b{b}_ffn1"; ffn_phase(b)
            S.scope = f"b{b}_ln2"; ln_phase(b, 1)
            S.scope = f"b{b}_attn"; attn_phase(b)
            S.scope = f"b{b}_ln3"; ln_phase(b, 2)
            S.scope = f"b{b}_ffn2"; ffn_phase(b)
            S.scope = f"b{b}_store"
            STQ = ("sync", "scalar", "gpsimd")
            for oc in range(NCH):
                w = ev_max(*[xev[("x", oc, i)] for i in range(NT)])
                res[("store", oc)] = S.E(STQ[oc % 3],
                    lambda e, b=b, oc=oc: e.dma_start(
                        out=Y[b, oc, :, :], in_=xs[:, 2048 * oc: 2048 * (oc + 1)]),
                    waits=w, inc=f"OU{oc}", n=16, dma=True)
            if b + 1 < NB:
                for ec in range(NCH):
                    last = None
                    for i in range(NT):
                        last = xload(b + 1, ec, i, waits=ev_max(res[("store", ec)]),
                                     q=STQ[ec % 3], sem=f"OU{ec}")
                    for i in range(NT):
                        xload_ev[(b + 1, ec, i)] = last
                for i in range(NT):
                    for ec in range(NCH):
                        xev.pop(("x", ec, i), None)

        S.E("sync", lambda e: e.nop(),
            waits=[(f"OU{oc}", S.tot.get(f"OU{oc}", 0)) for oc in range(8)]
                  + [("OU", S.tot.get("OU", 0))])

        with nc.Block() as block:
            @block.tensor
            def _(e):
                S.run("tensor", e, sems, nc)

            @block.vector
            def _(e):
                S.run("vector", e, sems, nc)

            @block.scalar
            def _(e):
                S.run("scalar", e, sems, nc)

            @block.gpsimd
            def _(e):
                S.run("gpsimd", e, sems, nc)

            @block.sync
            def _(e):
                S.run("sync", e, sems, nc)

    return nc


def _host_prep(inputs):
    x = np.asarray(inputs["L0_attn_x"], np.float32)
    Wk = np.asarray(inputs["L0_attn_Wk"], np.float32)
    Wq = np.asarray(inputs["L0_attn_Wq"], np.float32)
    Wv = np.asarray(inputs["L0_attn_Wv"], np.float32)
    W_out = np.asarray(inputs["L0_attn_W_out"], np.float32)
    b_out = np.asarray(inputs["L0_attn_b_out"], np.float32)
    g1r = np.asarray(inputs["L0_ffn_G1_R"], np.float32)
    g2r = np.asarray(inputs["L0_ffn_G2_R"], np.float32)
    g1i = np.asarray(inputs["L0_ffn_G1_I"], np.float32)
    g2i = np.asarray(inputs["L0_ffn_G2_I"], np.float32)
    br = np.asarray(inputs["L0_ffn_bias_real"], np.float32)
    bi = np.asarray(inputs["L0_ffn_bias_imag"], np.float32)

    com = {}
    com["WQ"] = np.ascontiguousarray(Wq.transpose(0, 2, 1))
    com["WK"] = np.ascontiguousarray(Wk.transpose(0, 2, 1))
    com["WV"] = np.ascontiguousarray(Wv.transpose(0, 2, 1))
    com["WO"] = np.ascontiguousarray(W_out.T)
    com["BO"] = np.ascontiguousarray(b_out.reshape(8, 128).T)
    com["BF"] = np.ascontiguousarray(np.concatenate([br, bi]).reshape(8, 128).T)
    com["G1T"] = np.ascontiguousarray(np.concatenate(
        [np.transpose(g1r[0], (1, 2, 0)), np.transpose(g1i[0], (1, 2, 0))], axis=0))
    com["G2T"] = np.ascontiguousarray(np.concatenate(
        [np.transpose(g2r[:, :, 0, :], (0, 2, 1)),
         np.transpose(g2i[:, :, 0, :], (0, 2, 1))], axis=0))
    com["SKA"] = np.kron(np.eye(32, dtype=np.float32), np.ones((1, 16), np.float32))
    com["SKB"] = np.tile(np.eye(16, dtype=np.float32), (1, 8))
    masks = np.empty((4, 128, 128), np.float32)
    for h, dil in enumerate(DILS):
        L = 16 // dil
        p = np.arange(128)
        jp, n = p // L, p % L
        jf, m = p // L, p % L
        allow = (jp[:, None] == jf[None, :]) & (m[None, :] <= n[:, None])
        masks[h] = np.where(allow, np.float32(0.0), np.float32(NEGINF))
    com["MASKS"] = masks
    com["IDENT"] = np.eye(128, dtype=np.float32)

    per_core = []
    for c in range(8):
        m = dict(com)
        m["X"] = np.ascontiguousarray(
            x[2 * c: 2 * c + 2].transpose(0, 2, 1).reshape(NB, 8, 128, T))
        per_core.append(m)
    return per_core


def kernel(**inputs):
    per_core = _host_prep(inputs)
    nc = build_program()
    res = run_bass_kernel_spmd(nc, per_core, core_ids=list(range(8)))
    outs = []
    for c in range(8):
        y = np.asarray(res.results[c]["Y"], np.float32)
        outs.append(y.reshape(NB, 1024, T).transpose(0, 2, 1))
    return np.ascontiguousarray(np.concatenate(outs, axis=0))
